# version 02
# baseline (speedup 1.0000x reference)
import contextlib
import numpy as np
import ml_dtypes
import concourse.bass as bass
import concourse.mybir as mybir
from concourse.bass_utils import run_bass_kernel_spmd

F32 = mybir.dt.float32
BF16 = mybir.dt.bfloat16
U8 = mybir.dt.uint8
AF = mybir.ActivationFunctionType
ALU = mybir.AluOpType
AX = mybir.AxisListType
BF16NP = ml_dtypes.bfloat16

B, S, D, H = 2, 4096, 2048, 5632
NH, NKV, DH, CHUNK, NCH = 16, 4, 128, 256, 16
DB = D // 128
HB = H // 128
HBH = 22
NSLOT = 8
SCALE = DH ** -0.5
EPS_LN = 1e-5
EPS_Q = 1e-8
BIG = 1e30
NUMEL_W = H * D

W12_L = 6
SEC_W1 = W12_L * 128 * D
SEC_W3 = 2 * 128 * H
NC_ELEM = 2 * SEC_W1 + SEC_W3
OFF_W1, OFF_W2, OFF_W3 = 0, SEC_W1, 2 * SEC_W1

SEC_K = 128 * NKV * 1024
SEC_V = 128 * NKV * NSLOT * 128
KV_ELEM = SEC_K + SEC_V

_UPTO = "all"

KB = 1024
OFF_PERSIST = 0
OFF_A = 16 * KB
OFF_B = 80 * KB
OFF_C = 112 * KB
ARENA_BYTES = 188 * KB


class Op:
    __slots__ = ("sem", "idx", "fn", "waits")

    def __init__(self, sem, idx, fn, waits):
        self.sem, self.idx, self.fn, self.waits = sem, idx, fn, waits


class Tk:
    STREAMS = ("pe", "act", "dve", "pool", "sync")
    QUEUE2STREAM = {"qsync": "sync", "qact": "act", "qdve": "pool", "qpool": "pool"}

    def __init__(self):
        self.ops = {s: [] for s in self.STREAMS}
        self.idx = {}
        self.last_w = {}
        self.readers = {}
        self.waited = {}
        self.needed = set()

    def stream_of(self, sem):
        if sem.startswith("dk_"):
            raise AssertionError("dk sems have explicit streams")
        return self.QUEUE2STREAM.get(sem, {"cc": "pool"}.get(sem, sem))

    @staticmethod
    def is_dma_sem(sem):
        return sem.startswith("dk_")

    def _edge(self, waits, consumer_stream, prod):
        pe_, pi = prod
        if pe_ == "pe" and consumer_stream == "pe":
            return
        key = (consumer_stream, pe_)
        if self.waited.get(key, 0) >= pi:
            return
        self.waited[key] = pi
        self.needed.add((pe_, pi))
        waits.append((pe_, pi))

    def _record(self, stream, sem, fn, reads, writes):
        self.idx[sem] = self.idx.get(sem, 0) + 1
        i = self.idx[sem]
        waits = []
        for k in reads:
            lw = self.last_w.get(k)
            if lw is not None:
                self._edge(waits, stream, lw)
        for k in writes:
            lw = self.last_w.get(k)
            if lw is not None:
                self._edge(waits, stream, lw)
            for r in self.readers.get(k, ()):
                self._edge(waits, stream, r)
        self.ops[stream].append(Op(sem, i, fn, waits))
        for k in reads:
            self.readers.setdefault(k, []).append((sem, i))
        for k in writes:
            self.last_w[k] = (sem, i)
            self.readers[k] = []

    def op(self, sem, fn, reads=(), writes=()):
        self._record(self.stream_of(sem), sem, fn, reads, writes)

    def dma(self, q, out, in_, reads=(), writes=()):
        assert writes, "dma needs a write key"
        sem = "dk_" + writes[0]
        self._record(self.QUEUE2STREAM[q], sem, lambda eng: eng.dma_start(out=out, in_=in_),
                     reads, writes)

    def all_sems(self):
        return list(self.idx.keys())

    def barrier(self):
        for stream in self.STREAMS:
            waits = []
            for sem_ in self.all_sems():
                src_stream = None if self.is_dma_sem(sem_) else self.stream_of(sem_)
                if self.idx[sem_] > 0 and src_stream != stream:
                    self._edge(waits, stream, (sem_, self.idx[sem_]))
            if waits:
                self.ops[stream].append(Op(None, 0, None, waits))

    def fence(self):
        waits = []
        for sem_ in self.all_sems():
            if self.idx[sem_] > 0 and (self.is_dma_sem(sem_) or self.stream_of(sem_) != "pool"):
                self._edge(waits, "pool", (sem_, self.idx[sem_]))
        self.ops["pool"].append(Op(None, 0, None, waits))

    def emit_stream(self, stream, eng, sems, prefix):
        for o in self.ops[stream]:
            for (pe_, pi) in o.waits:
                step = 16 if self.is_dma_sem(pe_) else 1
                eng.wait_ge(sems[pe_], prefix[pe_][pi] * step)
            if o.fn is None:
                continue
            inst = o.fn(eng)
            if self.is_dma_sem(o.sem) or (o.sem, o.idx) in self.needed:
                step = 16 if self.is_dma_sem(o.sem) else 1
                inst.then_inc(sems[o.sem], step)

    def prefixes(self):
        prefix = {}
        for sem_ in self.all_sems():
            n = self.idx[sem_]
            pc = [0] * (n + 1)
            for i in range(1, n + 1):
                pc[i] = pc[i - 1] + (1 if ((sem_, i) in self.needed or self.is_dma_sem(sem_)) else 0)
            prefix[sem_] = pc
        return prefix


def build_program(upto):
    nc = bass.Bass("TRN2", target_bir_lowering=False, debug=False)

    IN = {}
    def di(name, shape, dtype, kind="ExternalInput", **kw):
        IN[name] = nc.dram_tensor(name, shape, dtype, kind=kind, **kw).ap()
        return IN[name]

    di("x", [NSLOT * 128, D], F32)
    di("w1s", [W12_L * 128, D], F32)
    di("w2s", [W12_L * 128, D], F32)
    di("w3s", [256, H], F32)
    di("wq", [D, NH * DH], F32)
    di("wk", [D, NKV * DH], F32)
    di("wv", [D, NKV * DH], F32)
    di("wo", [NH * DH, D], F32)
    di("id_f", [128, 128], F32)
    di("ones_f", [128, 1], F32)
    di("kidx512", [128, 512], F32)
    di("rope_cos", [128, 1024], F32)
    di("rope_sin", [128, 1024], F32)
    di("thr_own", [128, NSLOT], F32)
    di("maskpre", [NSLOT * 128, NCH], F32)
    di("maskm1b", [NSLOT * 128, NCH], F32)
    di("onehot", [NSLOT * 128, NCH], F32)
    di("maskle", [NSLOT * 128, NCH], F32)
    di("y", [NSLOT * 128, D], F32, kind="ExternalOutput")
    di("gpart", [1, 4], F32, kind="Internal")
    di("gsum", [1, 4], F32, kind="Internal")
    di("contrib", [1, NC_ELEM], BF16, kind="Internal")
    di("gat", [8, NC_ELEM], BF16, kind="Internal")
    di("kvcon", [1, KV_ELEM], BF16, kind="Internal")
    di("kvgat", [4, KV_ELEM], BF16, kind="Internal")
    di("x_res", [D, 1024], F32, kind="Internal")

    tk = Tk()
    with contextlib.ExitStack() as stack:
        arena = stack.enter_context(nc.sbuf_tensor("arena", [128, ARENA_BYTES], U8))
        base_addr = nc.lookup_mloc(arena).addr

        SB = {}
        def place(name, shape, dtype, off):
            h = nc.virtual_sbuf_tensor(name, shape, dtype)
            nc._tensor(h.name, h.shape, h.dtype, type="SB")
            mloc = nc.lookup_mloc(h)
            mloc.allocated = True
            mloc.addr = base_addr + off
            SB[name] = h.ap()
            return SB[name]

        place("idf", [128, 128], F32, 0)
        place("idb", [128, 128], BF16, 512)
        place("onesb", [128, 128], BF16, 768)
        place("onesf", [128, 1], F32, 1024)
        place("kidx", [128, 512], F32, 1056)
        place("thr_sb", [128, NSLOT], F32, 3104)
        place("selc", [128, 4 * NSLOT * NCH], F32, 3136)
        place("th_all", [128, 4], F32, 5184)
        place("thneg", [128, 4], F32, 5200)
        place("gstat", [128, 2], F32, 5216)
        place("gpsb", [1, 4], F32, 5224)
        place("gssb", [128, 4], F32, 5240)
        place("kmb", [128, NKV * NCH], F32, 5256)
        place("kscr", [128, NKV * 32], F32, 5512)
        place("kmbb", [128, NKV * NCH], BF16, 6024)
        place("gsel", [128, NH * NCH], F32, 6152)
        place("m16", [128, NH], F32, 7176)
        place("eqb", [128, NH * NCH], F32, 7240)
        place("selacc", [128, NH * NCH], F32, 8264)
        place("dcol", [128, 16], F32, 9288)
        place("dv", [128, 1], F32, 9352)
        place("rv", [128, 1], F32, 9356)
        place("rrow", [1, 128], F32, 9360)
        place("rrowb", [1, 128], BF16, 9872)
        place("rbc_sb", [128, 128], F32, 10128)
        place("pts", [128, 2 * 128], BF16, 10640)
        place("sgst", [128, 1024], BF16, 11152)
        place("sqst", [128, 1024], BF16, 13200)

        place("qslab", [128, 2 * 2048], F32, OFF_A)
        place("qb12", [128, 2048], BF16, OFF_A + 16 * KB)
        place("tst12", [128, 2048], BF16, OFF_A + 20 * KB)
        place("negm", [128, 2048], BF16, OFF_A + 24 * KB)
        place("q3slab", [128, H], F32, OFF_A + 28 * KB)
        place("qb3", [128, H], BF16, OFF_A + 50 * KB + 512)
        place("tst3", [128, H], BF16, OFF_A + 61 * KB + 512 + 512)
        place("xinst", [128, 2 * 2048], F32, OFF_A)
        place("wb1", [128, 2 * 2048], BF16, OFF_A)
        place("wb2", [128, 2 * 2048], BF16, OFF_A + 8 * KB)
        place("wb3", [128, 2 * HBH * 128], BF16, OFF_A + 16 * KB)
        place("wqbh", [128, 8 * 2048], BF16, OFF_A)
        place("wkvb", [128, DB * 1024], BF16, OFF_A)
        place("ropec", [128, 1024], F32, OFF_A + 32 * KB)
        place("ropes", [128, 1024], F32, OFF_A + 36 * KB)
        place("wfst", [128, 2 * 2048], F32, OFF_A + 40 * KB)
        place("kTg", [128, 4 * 4096], BF16, OFF_A)
        place("vg", [128, 4 * 4096], BF16, OFF_A + 32 * KB)
        place("xfin", [128, 2 * 1024], F32, OFF_A)

        place("lnT", [128, DB * 1024], BF16, OFF_B)
        place("oT", [128, NH * 1024], BF16, OFF_B)

        place("xr", [128, DB * 1024], F32, OFF_C)
        place("gT", [128, HBH * 1024], BF16, OFF_C)
        place("QT", [128, NH * 1024], BF16, OFF_C)
        place("em", [128, 4096], BF16, OFF_C + 32 * KB)
        place("kT_loc", [128, NKV * 1024], BF16, OFF_C + 32 * KB)
        place("v_loc", [128, NKV * NSLOT * 128], BF16, OFF_C + 40 * KB)
        place("wob", [128, NH * 1024], BF16, OFF_C)
        place("xst_in", [128, 2 * 1024], F32, OFF_C + 48 * KB)
        place("xst_out", [128, 2 * 1024], F32, OFF_C + 56 * KB)
        place("scrA", [128, 1024], F32, OFF_C + 64 * KB)
        place("scrB", [128, 1024], F32, OFF_C + 68 * KB)
        place("mrow", [1, 1024], F32, OFF_C + 72 * KB)
        place("vrow", [1, 1024], F32, OFF_A)
        place("mrl", [1, 1024], F32, OFF_A + 4 * KB)
        place("ostage", [128, NSLOT * 2048], F32, OFF_C)

        PS = {}
        for i, nm in enumerate(("PS0", "PS1", "PS2", "PS3")):
            PS[nm] = stack.enter_context(nc.psum_tensor(nm, [128, 1024], F32))

        emit_body(nc, tk, IN, SB, PS, upto)
        tk.fence()
        prefix = tk.prefixes()
        sems = {s: stack.enter_context(nc.semaphore("sem_" + s)) for s in tk.all_sems()}
        with nc.Block() as block:
            block.tensor(lambda eng: tk.emit_stream("pe", eng, sems, prefix))
            block.scalar(lambda eng: tk.emit_stream("act", eng, sems, prefix))
            block.vector(lambda eng: tk.emit_stream("dve", eng, sems, prefix))
            block.gpsimd(lambda eng: tk.emit_stream("pool", eng, sems, prefix))
            block.sync(lambda eng: tk.emit_stream("sync", eng, sems, prefix))
    return nc


def emit_body(nc, tk, IN, SB, PS, upto):
    dma, op, bar = tk.dma, tk.op, tk.barrier
    idf, idb, onesb, onesf = SB["idf"], SB["idb"], SB["onesb"], SB["onesf"]
    PS0, PS1, PS2, PS3 = (PS["PS0"][:, :], PS["PS1"][:, :], PS["PS2"][:, :], PS["PS3"][:, :])
    lnT = SB["lnT"]
    xst_in, xst_out = SB["xst_in"], SB["xst_out"]

    dma("qsync", idf, IN["id_f"], writes=["idf"])
    dma("qsync", onesf, IN["ones_f"], writes=["onesf"])
    op("act", lambda e: e.copy(out=idb, in_=idf), reads=["idf"], writes=["idb"])
    op("dve", lambda e: e.memset(onesb, 1.0), writes=["onesb"])
    dma("qsync", SB["kidx"], IN["kidx512"], writes=["kidx"])
    dma("qsync", SB["thr_sb"], IN["thr_own"], writes=["thr_sb"])
    for w, nm in enumerate(("maskpre", "maskm1b", "onehot", "maskle")):
        dma("qsync", SB["selc"][:, w * NSLOT * NCH:(w + 1) * NSLOT * NCH]
            .rearrange("p (s c) -> p s c", s=NSLOT),
            IN[nm].rearrange("(s p) c -> p s c", p=128), writes=["selc"])

    qslab, q3slab = SB["qslab"], SB["q3slab"]
    gstat, gpsb, gssb = SB["gstat"], SB["gpsb"], SB["gssb"]
    th_all, thneg = SB["th_all"], SB["thneg"]
    cells = {"w1": (PS2[0:1, 0:1], "PS2a"), "w2": (PS2[0:1, 512:513], "PS2b"),
             "w3": (PS3[0:1, 0:1], "PS3a")}
    for wname, src in (("w1", IN["w1s"]), ("w2", IN["w2s"])):
        for l in range(W12_L):
            sl = qslab[:, (l % 2) * 2048:(l % 2) * 2048 + 2048]
            key = f"qs{l % 2}"
            dma("qsync", sl, src[l * 128:(l + 1) * 128, :], writes=[key])
            op("dve", lambda e, sl=sl: e.reduce_sum(
                out=gstat[:, 0:1], in_=sl, axis=AX.X, apply_absolute_value=True),
                reads=[key], writes=["gstat"])
            c, ck = cells[wname]
            op("pe", lambda e, c=c, l=l: e.matmul(
                out=c, lhsT=gstat[:, 0:1], rhs=onesf, start=(l == 0), stop=(l == W12_L - 1)),
                reads=["gstat", "onesf"], writes=[ck])
    for dl in range(2):
        dma("qsync", q3slab, IN["w3s"][dl * 128:(dl + 1) * 128, :], writes=["q3s"])
        op("dve", lambda e: e.reduce_sum(
            out=gstat[:, 1:2], in_=q3slab, axis=AX.X, apply_absolute_value=True),
            reads=["q3s"], writes=["gstat3"])
        c, ck = cells["w3"]
        op("pe", lambda e, c=c, dl=dl: e.matmul(
            out=c, lhsT=gstat[:, 1:2], rhs=onesf, start=(dl == 0), stop=(dl == 1)),
            reads=["gstat3", "onesf"], writes=[ck])
    for i, wname in enumerate(("w1", "w2", "w3")):
        c, ck = cells[wname]
        op("dve", lambda e, c=c, i=i: e.tensor_copy(out=gpsb[:, i:i + 1], in_=c),
           reads=[ck], writes=["gpsb"])
    op("dve", lambda e: e.tensor_copy(out=gpsb[:, 3:4], in_=gpsb[:, 2:3]),
       reads=["gpsb"], writes=["gpsb"])
    dma("qact", IN["gpart"], gpsb, reads=["gpsb"], writes=["gpart_d"])
    op("cc", lambda e: e.collective_compute(
        "AllReduce", ALU.add, replica_groups=[list(range(8))],
        ins=[IN["gpart"]], outs=[IN["gsum"]]), reads=["gpart_d"], writes=["gsum_d"])
    dma("qact", gssb, IN["gsum"].broadcast_to([128, 4]), reads=["gsum_d"], writes=["gssb"])
    op("dve", lambda e: e.tensor_scalar(
        out=th_all, in0=gssb, scalar1=0.5 / NUMEL_W, scalar2=0.5 * EPS_Q,
        op0=ALU.mult, op1=ALU.add), reads=["gssb"], writes=["th_all"])
    op("dve", lambda e: e.tensor_scalar_mul(out=thneg, in0=th_all, scalar1=-1.0),
       reads=["th_all"], writes=["thneg"])

    qb12, tst12, negm = SB["qb12"], SB["tst12"], SB["negm"]
    contrib = IN["contrib"]
    for wname, src, woff, wi in (("w1", IN["w1s"], OFF_W1, 0), ("w2", IN["w2s"], OFF_W2, 1)):
        for l in range(W12_L):
            sl = qslab[:, (l % 2) * 2048:(l % 2) * 2048 + 2048]
            key = f"qs{l % 2}"
            dma("qsync" if l % 2 == 0 else "qact", sl, src[l * 128:(l + 1) * 128, :], writes=[key])
            op("dve", lambda e, sl=sl, wi=wi: e.tensor_scalar(
                out=qb12, in0=sl, scalar1=th_all[:, wi:wi + 1], scalar2=None, op0=ALU.is_gt),
                reads=[key, "th_all"], writes=["qb12"])
            op("dve", lambda e, sl=sl, wi=wi: e.tensor_scalar(
                out=negm, in0=sl, scalar1=thneg[:, wi:wi + 1], scalar2=None, op0=ALU.is_lt),
                reads=[key, "thneg"], writes=["negm"])
            op("dve", lambda e: e.tensor_sub(out=qb12, in0=qb12, in1=negm),
                reads=["qb12", "negm"], writes=["qb12"])
            for k in range(DB):
                psv = (PS2[:, 0:64] if k % 2 == 0 else PS2[:, 512:576]).bitcast(BF16)
                pk = "PS2a" if k % 2 == 0 else "PS2b"
                op("pe", lambda e, k=k, psv=psv: e.transpose(
                    out=psv, in_=qb12[:, k * 128:(k + 1) * 128], identity=idb),
                    reads=["qb12", "idb"], writes=[pk])
                op("act", lambda e, k=k, psv=psv: e.copy(
                    out=tst12[:, k * 128:(k + 1) * 128], in_=psv),
                    reads=[pk], writes=["tst12"])
            dma("qact", contrib[0, woff + l * 128 * 2048: woff + (l + 1) * 128 * 2048]
                .rearrange("(p c) -> p c", p=128), tst12, reads=["tst12"], writes=["contrib_d"])
    qb3, tst3 = SB["qb3"], SB["tst3"]
    for dl in range(2):
        dma("qsync", q3slab, IN["w3s"][dl * 128:(dl + 1) * 128, :], writes=["q3s"])
        op("dve", lambda e: e.tensor_scalar(
            out=qb3, in0=q3slab, scalar1=th_all[:, 2:3], scalar2=None, op0=ALU.is_gt),
            reads=["q3s", "th_all"], writes=["qb3"])
        op("dve", lambda e: e.tensor_scalar(
            out=tst3, in0=q3slab, scalar1=thneg[:, 2:3], scalar2=None, op0=ALU.is_lt),
            reads=["q3s", "thneg"], writes=["tst3"])
        op("dve", lambda e: e.tensor_sub(out=qb3, in0=qb3, in1=tst3),
            reads=["qb3", "tst3"], writes=["qb3"])
        for hb in range(HB):
            psv = (PS2[:, 0:64] if hb % 2 == 0 else PS2[:, 512:576]).bitcast(BF16)
            pk = "PS2a" if hb % 2 == 0 else "PS2b"
            op("pe", lambda e, hb=hb, psv=psv: e.transpose(
                out=psv, in_=qb3[:, hb * 128:(hb + 1) * 128], identity=idb),
                reads=["qb3", "idb"], writes=[pk])
            op("act", lambda e, hb=hb, psv=psv: e.copy(
                out=tst3[:, hb * 128:(hb + 1) * 128], in_=psv),
                reads=[pk], writes=["tst3"])
        dma("qact", contrib[0, OFF_W3 + dl * 128 * H: OFF_W3 + (dl + 1) * 128 * H]
            .rearrange("(p c) -> p c", p=128), tst3, reads=["tst3"], writes=["contrib_d"])
    op("cc", lambda e: e.collective_compute(
        "AllGather", ALU.bypass, replica_groups=[list(range(8))],
        ins=[contrib], outs=[IN["gat"]]), reads=["contrib_d"], writes=["gat_d"])
    bar()

    xr, xinst = SB["xr"], SB["xinst"]
    for tt in range(NSLOT):
        sl = xinst[:, (tt % 2) * 2048:(tt % 2) * 2048 + 2048]
        key = f"xin{tt % 2}"
        dma("qdve", sl, IN["x"][tt * 128:(tt + 1) * 128, :], writes=[key])
        for k in range(DB):
            i2 = (tt * DB + k) % 2
            psv = PS2[:, 0:128] if i2 == 0 else PS2[:, 512:640]
            pk = "PS2a" if i2 == 0 else "PS2b"
            op("pe", lambda e, sl=sl, k=k, psv=psv: e.transpose(
                out=psv, in_=sl[:, k * 128:(k + 1) * 128], identity=idf),
                reads=[key, "idf"], writes=[pk])
            if i2 == 0:
                op("dve", lambda e, k=k, tt=tt, psv=psv: e.tensor_copy(
                    out=xr[:, k * 1024 + tt * 128: k * 1024 + tt * 128 + 128], in_=psv),
                    reads=[pk], writes=[f"xr{k}"])
            else:
                op("act", lambda e, k=k, tt=tt, psv=psv: e.copy(
                    out=xr[:, k * 1024 + tt * 128: k * 1024 + tt * 128 + 128], in_=psv),
                    reads=[pk], writes=[f"xr{k}"])
    for k in range(DB):
        dma("qact", IN["x_res"][k * 128:(k + 1) * 128, :], xr[:, k * 1024:(k + 1) * 1024],
            reads=[f"xr{k}"], writes=[f"xres{k}"])

    def emit_ln(tag, resident):
        sq = SB["sqst"]
        scell = [(PS2[0:1, 0:512], "PS2a"), (PS2[0:1, 512:1024], "PS2b"),
                 (PS3[0:1, 0:512], "PS3a"), (PS3[0:1, 512:1024], "PS3b")]
        for k in range(DB):
            if resident:
                srck, src = f"xr{k}", xr[:, k * 1024:(k + 1) * 1024]
            else:
                src = xst_in[:, (k % 2) * 1024:(k % 2) * 1024 + 1024]
                srck = f"lnst{k % 2}"
                dma("qdve", src, IN["x_res"][k * 128:(k + 1) * 128, :],
                    reads=[f"xres{k}"], writes=[srck])
            op("act", lambda e, src=src, k=k: e.copy(
                out=lnT[:, k * 1024:(k + 1) * 1024], in_=src),
                reads=[srck], writes=[f"lnT{k}"])
            op("act", lambda e, k=k: e.activation(
                out=sq, in_=lnT[:, k * 1024:(k + 1) * 1024], func=AF.Square),
                reads=[f"lnT{k}"], writes=["sqst"])
            for h2 in range(2):
                c, ck = scell[h2]
                op("pe", lambda e, k=k, h2=h2, c=c: e.matmul(
                    out=c, lhsT=onesb[:, 0:1],
                    rhs=lnT[:, k * 1024 + h2 * 512: k * 1024 + h2 * 512 + 512],
                    start=(k == 0), stop=(k == DB - 1)),
                    reads=[f"lnT{k}", "onesb"], writes=[ck])
                c, ck = scell[2 + h2]
                op("pe", lambda e, k=k, h2=h2, c=c: e.matmul(
                    out=c, lhsT=onesb[:, 0:1],
                    rhs=sq[:, h2 * 512: h2 * 512 + 512],
                    start=(k == 0), stop=(k == DB - 1)),
                    reads=["sqst", "onesb"], writes=[ck])
        mrow, vrow, mrl = SB["mrow"], SB["vrow"], SB["mrl"]
        op("dve", lambda e: e.tensor_scalar_mul(out=mrow, in0=PS2[0:1, 0:1024], scalar1=1.0 / D),
           reads=["PS2a", "PS2b"], writes=["mrow"])
        op("dve", lambda e: e.tensor_scalar_mul(out=vrow, in0=PS3[0:1, 0:1024], scalar1=1.0 / D),
           reads=["PS3a", "PS3b"], writes=["vrow"])
        op("dve", lambda e: e.tensor_mul(out=mrl, in0=mrow, in1=mrow),
           reads=["mrow"], writes=["mrl"])
        op("dve", lambda e: e.tensor_sub(out=vrow, in0=vrow, in1=mrl),
           reads=["vrow", "mrl"], writes=["vrow"])
        op("dve", lambda e: e.tensor_scalar_add(out=vrow, in0=vrow, scalar1=EPS_LN),
           reads=["vrow"], writes=["vrow"])
        op("act", lambda e: e.activation(out=vrow, in_=vrow, func=AF.Sqrt),
           reads=["vrow"], writes=["vrow"])
        op("dve", lambda e: e.reciprocal(out=vrow, in_=vrow),
           reads=["vrow"], writes=["vrow"])
        op("dve", lambda e: e.tensor_mul(out=mrl, in0=mrow, in1=vrow),
           reads=["mrow", "vrow"], writes=["mrl"])
        rlb = SB["sgst"].bitcast(BF16)[0:1, 0:1024]
        mlb = SB["sqst"].bitcast(BF16)[0:1, 0:1024]
        op("act", lambda e: e.copy(out=rlb, in_=vrow), reads=["vrow"], writes=["sgst"])
        op("act", lambda e: e.copy(out=mlb, in_=mrl), reads=["mrl"], writes=["sqst"])
        rbcl, mbcl = SB["scrA"], SB["scrB"]
        for h2 in range(2):
            c, ck = scell[h2]
            op("pe", lambda e, h2=h2: e.matmul(
                out=PS2[:, h2 * 512:h2 * 512 + 512], lhsT=onesb[0:1, :],
                rhs=rlb[:, h2 * 512:h2 * 512 + 512], start=True, stop=True),
                reads=["sgst", "onesb"], writes=["PS2a" if h2 == 0 else "PS2b"])
            op("pe", lambda e, h2=h2: e.matmul(
                out=PS3[:, h2 * 512:h2 * 512 + 512], lhsT=onesb[0:1, :],
                rhs=mlb[:, h2 * 512:h2 * 512 + 512], start=True, stop=True),
                reads=["sqst", "onesb"], writes=["PS3a" if h2 == 0 else "PS3b"])
        op("act", lambda e: e.copy(out=rbcl, in_=PS2[:, 0:1024]),
           reads=["PS2a", "PS2b"], writes=["scrA"])
        op("act", lambda e: e.copy(out=mbcl, in_=PS3[:, 0:1024]),
           reads=["PS3a", "PS3b"], writes=["scrB"])
        for k in range(DB):
            op("dve", lambda e, k=k: e.tensor_mul(
                out=lnT[:, k * 1024:(k + 1) * 1024],
                in0=lnT[:, k * 1024:(k + 1) * 1024], in1=rbcl),
                reads=[f"lnT{k}", "scrA"], writes=[f"lnT{k}"])
            op("pool", lambda e, k=k: e.tensor_sub(
                out=lnT[:, k * 1024:(k + 1) * 1024],
                in0=lnT[:, k * 1024:(k + 1) * 1024], in1=mbcl),
                reads=[f"lnT{k}", "scrB"], writes=[f"lnT{k}"])

    def emit_ffn(tag):
        wb1, wb2, wb3, sgst, gT = SB["wb1"], SB["wb2"], SB["wb3"], SB["sgst"], SB["gT"]
        gat = IN["gat"]
        for half in range(2):
            for hh in range(HBH):
                hb = half * HBH + hh
                r, l = hb % 8, hb // 8
                bsel = hh % 2
                w1sl = wb1[:, bsel * 2048:bsel * 2048 + 2048]
                w2sl = wb2[:, bsel * 2048:bsel * 2048 + 2048]
                k1, k2 = f"wb1_{bsel}", f"wb2_{bsel}"
                off1 = OFF_W1 + l * 128 * 2048
                off2 = OFF_W2 + l * 128 * 2048
                dma("qsync", w1sl, gat[r, off1:off1 + 128 * 2048].rearrange("(p c) -> p c", p=128),
                    reads=["gat_d"], writes=[k1])
                dma("qact", w2sl, gat[r, off2:off2 + 128 * 2048].rearrange("(p c) -> p c", p=128),
                    reads=["gat_d"], writes=[k2])
                u1 = [PS0, PS2][bsel]
                u2 = [PS1, PS3][bsel]
                ku1 = ["PS0", "PS2"][bsel]
                ku2 = ["PS1", "PS3"][bsel]
                for k in range(DB):
                    for h2 in range(2):
                        op("pe", lambda e, k=k, h2=h2, u1=u1, w1sl=w1sl: e.matmul(
                            out=u1[:, h2 * 512:h2 * 512 + 512],
                            lhsT=w1sl[:, k * 128:(k + 1) * 128],
                            rhs=lnT[:, k * 1024 + h2 * 512: k * 1024 + h2 * 512 + 512],
                            start=(k == 0), stop=(k == DB - 1)),
                            reads=[k1, f"lnT{k}"], writes=[ku1 + ("a" if h2 == 0 else "b")])
                        op("pe", lambda e, k=k, h2=h2, u2=u2, w2sl=w2sl: e.matmul(
                            out=u2[:, h2 * 512:h2 * 512 + 512],
                            lhsT=w2sl[:, k * 128:(k + 1) * 128],
                            rhs=lnT[:, k * 1024 + h2 * 512: k * 1024 + h2 * 512 + 512],
                            start=(k == 0), stop=(k == DB - 1)),
                            reads=[k2, f"lnT{k}"], writes=[ku2 + ("a" if h2 == 0 else "b")])
                op("act", lambda e, u1=u1: e.activation(
                    out=sgst, in_=u1, func=AF.Sigmoid),
                    reads=[ku1 + "a", ku1 + "b"], writes=["sgst"])
                op("dve", lambda e, hh=hh, u2=u2: e.tensor_mul(
                    out=gT[:, hh * 1024:(hh + 1) * 1024], in0=sgst, in1=u2),
                    reads=["sgst", ku2 + "a", ku2 + "b"], writes=[f"gT{hh}"])
            for k in range(DB):
                r, dl = k // 2, k % 2
                bsel = k % 2
                y = [PS0, PS1][bsel]
                ky = ["PS0", "PS1"][bsel]
                w3sl = wb3[:, bsel * HBH * 128:(bsel + 1) * HBH * 128]
                k3 = f"wb3_{bsel}"
                src = gat[r, OFF_W3 + dl * 128 * H: OFF_W3 + (dl + 1) * 128 * H] \
                    .rearrange("(p hb c) -> p hb c", p=128, hb=HB)[:, half * HBH:(half + 1) * HBH, :]
                dma("qsync" if bsel == 0 else "qact", w3sl, src, reads=["gat_d"], writes=[k3])
                for hh in range(HBH):
                    for h2 in range(2):
                        op("pe", lambda e, hh=hh, h2=h2, y=y, w3sl=w3sl: e.matmul(
                            out=y[:, h2 * 512:h2 * 512 + 512],
                            lhsT=w3sl[:, hh * 128:(hh + 1) * 128],
                            rhs=gT[:, hh * 1024 + h2 * 512: hh * 1024 + h2 * 512 + 512],
                            start=(hh == 0), stop=(hh == HBH - 1)),
                            reads=[k3, f"gT{hh}"], writes=[ky + ("a" if h2 == 0 else "b")])
                xi = xst_in[:, bsel * 1024:bsel * 1024 + 1024]
                xo = xst_out[:, bsel * 1024:bsel * 1024 + 1024]
                kxi, kxo = f"xi{bsel}", f"xo{bsel}"
                dma("qdve", xi, IN["x_res"][k * 128:(k + 1) * 128, :],
                    reads=[f"xres{k}"], writes=[kxi])
                op("dve", lambda e, y=y, xi=xi, xo=xo: e.scalar_tensor_tensor(
                    out=xo, in0=y, scalar=0.5, in1=xi, op0=ALU.mult, op1=ALU.add),
                    reads=[ky + "a", ky + "b", kxi], writes=[kxo])
                dma("qact", IN["x_res"][k * 128:(k + 1) * 128, :], xo,
                    reads=[kxo], writes=[f"xres{k}"])
            bar()

    def emit_attn():
        ropec, ropes = SB["ropec"], SB["ropes"]
        dma("qsync", ropec, IN["rope_cos"], writes=["ropec"])
        dma("qsync", ropes, IN["rope_sin"], writes=["ropes"])
        wfst, QT = SB["wfst"], SB["QT"]
        scrA, scrB = SB["scrA"], SB["scrB"]

        def rope_evac(psrc, kps, dstap, dstkey):
            op("act", lambda e: e.copy(out=scrB[0:64, :], in_=psrc[64:128, :]),
               reads=kps, writes=["scrB"])
            op("act", lambda e: e.copy(out=scrB[64:128, :], in_=psrc[0:64, :]),
               reads=kps, writes=["scrB"])
            op("dve", lambda e: e.tensor_mul(out=scrB, in0=scrB, in1=ropes),
               reads=["scrB", "ropes"], writes=["scrB"])
            op("dve", lambda e: e.tensor_mul(out=scrA, in0=psrc, in1=ropec),
               reads=kps + ["ropec"], writes=["scrA"])
            op("pool", lambda e: e.tensor_add(out=dstap, in0=scrA, in1=scrB),
               reads=["scrA", "scrB"], writes=[dstkey])

        wqbh = SB["wqbh"]
        for hg in range(2):
            for k in range(DB):
                sl = wfst[:, (k % 2) * 2048:(k % 2) * 2048 + 1024]
                key = f"wf{k % 2}"
                dma("qsync" if k % 2 == 0 else "qact", sl,
                    IN["wq"][k * 128:(k + 1) * 128, hg * 1024:(hg + 1) * 1024],
                    writes=[key])
                op("pool", lambda e, sl=sl, k=k: e.tensor_copy(
                    out=wqbh[:, k * 1024:(k + 1) * 1024], in_=sl),
                    reads=[key], writes=[f"wqb{k}"])
            for h8 in range(8):
                h = hg * 8 + h8
                qps = [PS0, PS1][h8 % 2]
                kq = ["PS0", "PS1"][h8 % 2]
                for k in range(DB):
                    for h2 in range(2):
                        op("pe", lambda e, h8=h8, k=k, h2=h2, qps=qps: e.matmul(
                            out=qps[:, h2 * 512:h2 * 512 + 512],
                            lhsT=wqbh[:, k * 1024 + h8 * 128: k * 1024 + h8 * 128 + 128],
                            rhs=lnT[:, k * 1024 + h2 * 512: k * 1024 + h2 * 512 + 512],
                            start=(k == 0), stop=(k == DB - 1)),
                            reads=[f"wqb{k}", f"lnT{k}"],
                            writes=[kq + ("a" if h2 == 0 else "b")])
                rope_evac(qps, [kq + "a", kq + "b"], QT[:, h * 1024:(h + 1) * 1024], f"QT{h}")
        bar()
        wkvb = SB["wkvb"]
        for k in range(DB):
            sl = wfst[:, (k % 2) * 2048:(k % 2) * 2048 + 1024]
            key = f"wf{k % 2}"
            dma("qsync", sl[:, 0:512], IN["wk"][k * 128:(k + 1) * 128, :], writes=[key])
            dma("qsync", sl[:, 512:1024], IN["wv"][k * 128:(k + 1) * 128, :], writes=[key])
            op("pool", lambda e, sl=sl, k=k: e.tensor_copy(
                out=wkvb[:, k * 1024:(k + 1) * 1024], in_=sl), reads=[key], writes=[f"wkv{k}"])
        kT_loc, v_loc = SB["kT_loc"], SB["v_loc"]
        for kv in range(NKV):
            kps = [PS0, PS1][kv % 2]
            kk = ["PS0", "PS1"][kv % 2]
            for k in range(DB):
                for h2 in range(2):
                    op("pe", lambda e, kv=kv, k=k, h2=h2, kps=kps: e.matmul(
                        out=kps[:, h2 * 512:h2 * 512 + 512],
                        lhsT=wkvb[:, k * 1024 + kv * 128: k * 1024 + kv * 128 + 128],
                        rhs=lnT[:, k * 1024 + h2 * 512: k * 1024 + h2 * 512 + 512],
                        start=(k == 0), stop=(k == DB - 1)),
                        reads=[f"wkv{k}", f"lnT{k}"], writes=[kk + ("a" if h2 == 0 else "b")])
            rope_evac(kps, [kk + "a", kk + "b"], kT_loc[:, kv * 1024:(kv + 1) * 1024], f"kTl{kv}")
        for kv in range(NKV):
            for tt in range(NSLOT):
                i2 = (kv * NSLOT + tt) % 2
                vps = PS2[:, 0:128] if i2 == 0 else PS2[:, 512:640]
                kvp = "PS2a" if i2 == 0 else "PS2b"
                for k in range(DB):
                    op("pe", lambda e, kv=kv, tt=tt, k=k, vps=vps: e.matmul(
                        out=vps,
                        lhsT=lnT[:, k * 1024 + tt * 128: k * 1024 + tt * 128 + 128],
                        rhs=wkvb[:, k * 1024 + 512 + kv * 128: k * 1024 + 512 + kv * 128 + 128],
                        start=(k == 0), stop=(k == DB - 1)),
                        reads=[f"wkv{k}", f"lnT{k}"], writes=[kvp])
                op("act", lambda e, kv=kv, tt=tt, vps=vps: e.copy(
                    out=v_loc[:, (kv * NSLOT + tt) * 128:(kv * NSLOT + tt) * 128 + 128],
                    in_=vps), reads=[kvp], writes=[f"vl{kv}"])
        kvcon = IN["kvcon"]
        dma("qact", kvcon[0, 0:SEC_K].rearrange("(p c) -> p c", p=128), kT_loc,
            reads=["kTl0", "kTl1", "kTl2", "kTl3"], writes=["kvcon_d"])
        dma("qact", kvcon[0, SEC_K:KV_ELEM].rearrange("(p c) -> p c", p=128), v_loc,
            reads=["vl0", "vl1", "vl2", "vl3"], writes=["kvcon_d"])
        op("cc", lambda e: e.collective_compute(
            "AllGather", ALU.bypass, replica_groups=[[0, 1, 2, 3], [4, 5, 6, 7]],
            ins=[kvcon], outs=[IN["kvgat"]]), reads=["kvcon_d"], writes=["kvg_d"])
        bar()
        kTg, vg = SB["kTg"], SB["vg"]
        for r in range(4):
            dma("qsync", kTg[:, r * 4096:(r + 1) * 4096],
                IN["kvgat"][r, 0:SEC_K].rearrange("(p c) -> p c", p=128),
                reads=["kvg_d"], writes=["kTg"])
            dma("qact", vg[:, r * 4096:(r + 1) * 4096],
                IN["kvgat"][r, SEC_K:KV_ELEM].rearrange("(p c) -> p c", p=128),
                reads=["kvg_d"], writes=["vg"])
        kmb, kscr, kmbb = SB["kmb"], SB["kscr"], SB["kmbb"]
        kview = kTg.rearrange("p (r kv l c) -> p kv l r c", r=4, kv=NKV, l=8)
        for kv in range(NKV):
            for l in range(8):
                op("dve", lambda e, kv=kv, l=l: e.reduce_sum(
                    out=kscr[:, (kv * 8 + l) * 4:(kv * 8 + l) * 4 + 4],
                    in_=kview[:, kv, l, :, :], axis=AX.X),
                    reads=["kTg"], writes=["kscr"])
        kv4 = kscr.rearrange("p (x a b) -> p x a b", a=2, b=2)
        op("dve", lambda e: e.tensor_add(
            out=kmb.rearrange("p (x a) -> p x a", a=2),
            in0=kv4[:, :, :, 0], in1=kv4[:, :, :, 1]),
            reads=["kscr"], writes=["kmb"])
        op("act", lambda e: e.copy(out=kmbb, in_=kmb), reads=["kmb"], writes=["kmbb"])

        selc = SB["selc"]
        gsel, m16, eqb, selacc = SB["gsel"], SB["m16"], SB["eqb"], SB["selacc"]
        dcol, dv, rv = SB["dcol"], SB["dv"], SB["rv"]
        em, oT = SB["em"], SB["oT"]
        kidx, thr_sb = SB["kidx"], SB["thr_sb"]
        gv = gsel.rearrange("p (h c) -> p h c", h=NH)
        ev = eqb.rearrange("p (h c) -> p h c", h=NH)
        sv3 = selacc.rearrange("p (h c) -> p h c", h=NH)
        for i in range(NSLOT):
            gate_ps = PS2[:, 0:256]
            for h in range(NH):
                kv = h // 4
                op("pe", lambda e, i=i, h=h, kv=kv: e.matmul(
                    out=gate_ps[:, h * NCH:(h + 1) * NCH],
                    lhsT=QT[:, h * 1024 + i * 128: h * 1024 + i * 128 + 128],
                    rhs=kmbb[:, kv * NCH:(kv + 1) * NCH],
                    start=True, stop=True, skip_group_check=True),
                    reads=[f"QT{h}", "kmbb"], writes=["PS2a"])
            mpre = selc[:, (0 * NSLOT + i) * NCH:(0 * NSLOT + i + 1) * NCH]
            m1b = selc[:, (1 * NSLOT + i) * NCH:(1 * NSLOT + i + 1) * NCH]
            moh = selc[:, (2 * NSLOT + i) * NCH:(2 * NSLOT + i + 1) * NCH]
            mle = selc[:, (3 * NSLOT + i) * NCH:(3 * NSLOT + i + 1) * NCH]
            op("dve", lambda e, mpre=mpre, gate_ps=gate_ps: e.tensor_tensor(
                out=gv, in0=gate_ps.rearrange("p (h c) -> p h c", h=NH),
                in1=mpre[:, None, :].broadcast_to([128, NH, NCH]), op=ALU.mult),
                reads=["PS2a", "selc"], writes=["gsel"])
            op("dve", lambda e, m1b=m1b: e.tensor_tensor(
                out=gv, in0=gv, in1=m1b[:, None, :].broadcast_to([128, NH, NCH]),
                op=ALU.add), reads=["gsel", "selc"], writes=["gsel"])
            for it in range(3):
                op("dve", lambda e: e.reduce_max(out=m16, in_=gv, axis=AX.X),
                   reads=["gsel"], writes=["m16"])
                op("dve", lambda e: e.tensor_tensor(
                    out=ev, in0=gv, in1=m16[:, :, None].broadcast_to([128, NH, NCH]),
                    op=ALU.is_ge), reads=["gsel", "m16"], writes=["eqb"])
                if it == 0:
                    op("dve", lambda e: e.tensor_copy(out=selacc, in_=eqb),
                       reads=["eqb"], writes=["selacc"])
                else:
                    op("dve", lambda e: e.tensor_add(out=selacc, in0=selacc, in1=eqb),
                       reads=["selacc", "eqb"], writes=["selacc"])
                if it < 2:
                    op("dve", lambda e: e.scalar_tensor_tensor(
                        out=gsel, in0=eqb, scalar=-BIG, in1=gsel, op0=ALU.mult, op1=ALU.add),
                        reads=["eqb", "gsel"], writes=["gsel"])
            op("pool", lambda e, moh=moh: e.tensor_tensor(
                out=sv3, in0=sv3, in1=moh[:, None, :].broadcast_to([128, NH, NCH]),
                op=ALU.add), reads=["selacc", "selc"], writes=["selacc"])
            op("dve", lambda e: e.tensor_scalar_min(out=selacc, in0=selacc, scalar1=1.0),
               reads=["selacc"], writes=["selacc"])
            op("pool", lambda e, mle=mle: e.tensor_tensor(
                out=sv3, in0=sv3, in1=mle[:, None, :].broadcast_to([128, NH, NCH]),
                op=ALU.mult), reads=["selacc", "selc"], writes=["selacc"])
            nkb = i + 1
            nkt = 4 * (i + 1)
            for h in range(NH):
                kv = h // 4
                scps = [(PS0[:, 0:512], "PS0a"), (PS0[:, 512:1024], "PS0b"),
                        (PS1[:, 0:512], "PS1a"), (PS1[:, 512:1024], "PS1b")]
                for kb in range(nkb):
                    sps, ksp = scps[kb % 4]
                    rhs = kview[:, kv, kb, :, :]
                    op("pe", lambda e, i=i, h=h, sps=sps, rhs=rhs: e.matmul(
                        out=sps, lhsT=QT[:, h * 1024 + i * 128: h * 1024 + i * 128 + 128],
                        rhs=rhs, start=True, stop=True),
                        reads=[f"QT{h}", "kTg"], writes=[ksp])
                    op("act", lambda e, kb=kb, sps=sps: e.activation(
                        out=em[:, kb * 512:(kb + 1) * 512], in_=sps, func=AF.Exp, scale=SCALE),
                        reads=[ksp], writes=["em"])
                op("dve", lambda e, i=i: e.scalar_tensor_tensor(
                    out=em[:, i * 512:(i + 1) * 512], in0=kidx,
                    scalar=thr_sb[:, i:i + 1],
                    in1=em[:, i * 512:(i + 1) * 512], op0=ALU.is_le, op1=ALU.mult),
                    reads=["em", "thr_sb", "kidx"], writes=["em"])
                for kb in range(nkb):
                    sv = sv3[:, h, 2 * kb:2 * kb + 2]
                    op("dve", lambda e, kb=kb, sv=sv: e.scalar_tensor_tensor(
                        out=em[:, kb * 512:(kb + 1) * 512].rearrange("p (c j) -> p c j", c=2),
                        in0=em[:, kb * 512:(kb + 1) * 512].rearrange("p (c j) -> p c j", c=2),
                        scalar=1.0, in1=sv[:, :, None].broadcast_to([128, 2, 256]),
                        op0=ALU.mult, op1=ALU.mult,
                        accum_out=dcol[:, kb:kb + 1]),
                        reads=["em", "selacc"], writes=["em", "dcolk"])
                op("dve", lambda e, nkb=nkb: e.reduce_sum(out=dv, in_=dcol[:, 0:nkb], axis=AX.X),
                   reads=["dcolk"], writes=["dv"])
                op("dve", lambda e: e.reciprocal(out=rv, in_=dv), reads=["dv"], writes=["rv"])
                op("pe", lambda e: e.transpose(out=PS3[0:1, 128:256], in_=rv, identity=idf),
                   reads=["rv", "idf"], writes=["PS3a"])
                op("dve", lambda e: e.tensor_copy(out=SB["rrow"], in_=PS3[0:1, 128:256]),
                   reads=["PS3a"], writes=["rrow"])
                op("act", lambda e: e.copy(out=SB["rrowb"], in_=SB["rrow"]),
                   reads=["rrow"], writes=["rrowb"])
                op("pe", lambda e: e.matmul(
                    out=PS3[:, 512:640], lhsT=onesb[0:1, :], rhs=SB["rrowb"],
                    start=True, stop=True), reads=["rrowb", "onesb"], writes=["PS3b"])
                op("act", lambda e: e.copy(out=SB["rbc_sb"], in_=PS3[:, 512:640]),
                   reads=["PS3b"], writes=["rbc_sb"])
                pts = SB["pts"]
                for kt in range(nkt):
                    i2 = kt % 2
                    ppsv = (PS2[:, 0:64] if i2 == 0 else PS2[:, 512:576]).bitcast(BF16)
                    kpp = "PS2a" if i2 == 0 else "PS2b"
                    op("pe", lambda e, kt=kt, ppsv=ppsv: e.transpose(
                        out=ppsv, in_=em[:, kt * 128:(kt + 1) * 128], identity=idb),
                        reads=["em", "idb"], writes=[kpp])
                    pt = pts[:, i2 * 128:i2 * 128 + 128]
                    kpt = f"pts{i2}"
                    if i2 == 0:
                        op("act", lambda e, ppsv=ppsv, pt=pt: e.copy(out=pt, in_=ppsv),
                           reads=[kpp], writes=[kpt])
                    else:
                        op("dve", lambda e, ppsv=ppsv, pt=pt: e.tensor_copy(out=pt, in_=ppsv),
                           reads=[kpp], writes=[kpt])
                    r_, l_ = kt % 4, kt // 4
                    vt = vg[:, r_ * 4096 + kv * 1024 + l_ * 128:
                            r_ * 4096 + kv * 1024 + l_ * 128 + 128]
                    op("pe", lambda e, kt=kt, vt=vt, pt=pt, nkt=nkt: e.matmul(
                        out=PS3[:, 0:128], lhsT=vt, rhs=pt,
                        start=(kt == 0), stop=(kt == nkt - 1)),
                        reads=["vg", kpt], writes=["PS3a"])
                op("dve", lambda e, h=h, i=i: e.tensor_mul(
                    out=oT[:, h * 1024 + i * 128: h * 1024 + i * 128 + 128],
                    in0=PS3[:, 0:128], in1=SB["rbc_sb"]),
                    reads=["PS3a", "rbc_sb"], writes=[f"oT{h}"])
        bar()
        wob = SB["wob"]
        for dh2 in range(2):
            for h in range(NH):
                sl = wfst[:, (h % 2) * 2048:(h % 2) * 2048 + 1024]
                key = f"wf{h % 2}"
                dma("qsync" if h % 2 == 0 else "qact", sl,
                    IN["wo"][h * 128:(h + 1) * 128, dh2 * 1024:(dh2 + 1) * 1024],
                    writes=[key])
                op("pool", lambda e, sl=sl, h=h: e.tensor_copy(
                    out=wob[:, h * 1024:(h + 1) * 1024], in_=sl),
                    reads=[key], writes=[f"wob{h}"])
            for k8 in range(8):
                k = dh2 * 8 + k8
                bsel = k8 % 2
                aps = [PS0, PS1][bsel]
                kap = ["PS0", "PS1"][bsel]
                for h in range(NH):
                    for h2 in range(2):
                        op("pe", lambda e, h=h, h2=h2, k8=k8, aps=aps: e.matmul(
                            out=aps[:, h2 * 512:h2 * 512 + 512],
                            lhsT=wob[:, h * 1024 + k8 * 128: h * 1024 + k8 * 128 + 128],
                            rhs=oT[:, h * 1024 + h2 * 512: h * 1024 + h2 * 512 + 512],
                            start=(h == 0), stop=(h == NH - 1)),
                            reads=[f"wob{h}", f"oT{h}"],
                            writes=[kap + ("a" if h2 == 0 else "b")])
                xi = xst_in[:, bsel * 1024:bsel * 1024 + 1024]
                xo = xst_out[:, bsel * 1024:bsel * 1024 + 1024]
                kxi, kxo = f"xi{bsel}", f"xo{bsel}"
                dma("qdve", xi, IN["x_res"][k * 128:(k + 1) * 128, :],
                    reads=[f"xres{k}"], writes=[kxi])
                op("dve", lambda e, aps=aps, xi=xi, xo=xo: e.scalar_tensor_tensor(
                    out=xo, in0=aps, scalar=1.0, in1=xi, op0=ALU.mult, op1=ALU.add),
                    reads=[kap + "a", kap + "b", kxi], writes=[kxo])
                dma("qact", IN["x_res"][k * 128:(k + 1) * 128, :], xo,
                    reads=[kxo], writes=[f"xres{k}"])

    def emit_out():
        ostage, xfin = SB["ostage"], SB["xfin"]
        for k in range(DB):
            xi = xfin[:, (k % 2) * 1024:(k % 2) * 1024 + 1024]
            kxi = f"fxi{k % 2}"
            dma("qdve", xi, IN["x_res"][k * 128:(k + 1) * 128, :],
                reads=[f"xres{k}"], writes=[kxi])
            for tt in range(NSLOT):
                i2 = (k * NSLOT + tt) % 2
                psv = PS2[:, 0:128] if i2 == 0 else PS2[:, 512:640]
                pk = "PS2a" if i2 == 0 else "PS2b"
                op("pe", lambda e, xi=xi, tt=tt, psv=psv: e.transpose(
                    out=psv, in_=xi[:, tt * 128:(tt + 1) * 128], identity=idf),
                    reads=[kxi, "idf"], writes=[pk])
                if i2 == 0:
                    op("act", lambda e, tt=tt, k=k, psv=psv: e.copy(
                        out=ostage[:, tt * 2048 + k * 128: tt * 2048 + k * 128 + 128],
                        in_=psv), reads=[pk], writes=[f"ost{tt}"])
                else:
                    op("dve", lambda e, tt=tt, k=k, psv=psv: e.tensor_copy(
                        out=ostage[:, tt * 2048 + k * 128: tt * 2048 + k * 128 + 128],
                        in_=psv), reads=[pk], writes=[f"ost{tt}"])
        for tt in range(NSLOT):
            dma("qact", IN["y"][tt * 128:(tt + 1) * 128, :],
                ostage[:, tt * 2048:(tt + 1) * 2048], reads=[f"ost{tt}"], writes=["y_d"])

    emit_ln("l1", resident=True)
    bar()
    emit_ffn("f1")
    bar()
    if upto != "ffn1":
        emit_ln("l2", resident=False)
        bar()
        emit_attn()
        bar()
    if upto == "all":
        emit_ln("l3", resident=False)
        bar()
        emit_ffn("f2")
        bar()
    emit_out()



def _host_constants(core):
    b, j = core // 4, core % 4
    tiles = [4 * i + j for i in range(NSLOT)]
    pos = np.concatenate([np.arange(t * 128, (t + 1) * 128) for t in tiles])
    ident = np.eye(128, dtype=np.float32)
    out = {
        "id_f": ident,
        "ones_f": np.ones((128, 1), dtype=np.float32),
        "kidx512": np.broadcast_to(np.arange(512, dtype=np.float32), (128, 512)).copy(),
    }
    invf = (1.0 / (10000.0 ** (np.arange(0, DH, 2, dtype=np.float64) / DH)))
    ang = pos.astype(np.float64)[None, :] * invf[:, None]
    cos = np.cos(ang)
    sin = np.sin(ang)
    out["rope_cos"] = np.concatenate([cos, cos], 0).astype(np.float32)
    rs = np.concatenate([-sin, sin], 0).astype(np.float32)
    out["rope_sin"] = rs
    thr = np.zeros((128, NSLOT), dtype=np.float32)
    for i in range(NSLOT):
        thr[:, i] = pos[i * 128:(i + 1) * 128] - 512.0 * i
    out["thr_own"] = thr
    mpre = np.zeros((NSLOT * 128, NCH), dtype=np.float32)
    m1b = np.zeros((NSLOT * 128, NCH), dtype=np.float32)
    moh = np.zeros((NSLOT * 128, NCH), dtype=np.float32)
    mle = np.zeros((NSLOT * 128, NCH), dtype=np.float32)
    for i in range(NSLOT):
        qc = tiles[i] // 2
        c = np.arange(NCH)
        mpre[i * 128:(i + 1) * 128, :] = (c < qc).astype(np.float32)[None, :]
        m1b[i * 128:(i + 1) * 128, :] = (((c < qc).astype(np.float32) - 1.0) * BIG)[None, :]
        moh[i * 128:(i + 1) * 128, :] = (c == qc).astype(np.float32)[None, :]
        mle[i * 128:(i + 1) * 128, :] = (c <= qc).astype(np.float32)[None, :]
    out["maskpre"] = mpre
    out["maskm1b"] = m1b
    out["onehot"] = moh
    out["maskle"] = mle
    return out, tiles


def kernel(L0_attn_hidden_states, L0_attn_wq, L0_attn_wk, L0_attn_wv, L0_attn_wo,
           L0_ffn_w1, L0_ffn_w2, L0_ffn_w3):
    x = np.ascontiguousarray(np.asarray(L0_attn_hidden_states, dtype=np.float32))
    wq = np.ascontiguousarray(np.asarray(L0_attn_wq, dtype=np.float32))
    wk = np.ascontiguousarray(np.asarray(L0_attn_wk, dtype=np.float32))
    wv = np.ascontiguousarray(np.asarray(L0_attn_wv, dtype=np.float32))
    wo = np.ascontiguousarray(np.asarray(L0_attn_wo, dtype=np.float32))
    w1 = np.ascontiguousarray(np.asarray(L0_ffn_w1, dtype=np.float32))
    w2 = np.ascontiguousarray(np.asarray(L0_ffn_w2, dtype=np.float32))
    w3 = np.ascontiguousarray(np.asarray(L0_ffn_w3, dtype=np.float32))

    nc = build_program(_UPTO)
    in_maps = []
    tile_maps = []
    for core in range(8):
        b, j = core // 4, core % 4
        consts, tiles = _host_constants(core)
        tile_maps.append((b, tiles))
        xs = np.concatenate([x[b, t * 128:(t + 1) * 128, :] for t in tiles], axis=0)
        def w12_slice(w):
            rows = []
            for l in range(W12_L):
                jj = l * 8 + core
                if jj < HB:
                    rows.append(w[jj * 128:(jj + 1) * 128, :])
                else:
                    rows.append(np.zeros((128, w.shape[1]), dtype=np.float32))
            return np.ascontiguousarray(np.concatenate(rows, axis=0))
        m = {
            "x": np.ascontiguousarray(xs),
            "w1s": w12_slice(w1),
            "w2s": w12_slice(w2),
            "w3s": np.ascontiguousarray(w3[core * 256:(core + 1) * 256, :]),
            "wq": wq, "wk": wk, "wv": wv, "wo": wo,
        }
        m.update(consts)
        in_maps.append(m)
    res = run_bass_kernel_spmd(nc, in_maps, core_ids=list(range(8)))
    out = np.zeros((B, S, D), dtype=np.float32)
    for core in range(8):
        b, tiles = tile_maps[core]
        yc = np.asarray(res.results[core]["y"])
        for i, t in enumerate(tiles):
            out[b, t * 128:(t + 1) * 128, :] = yc[i * 128:(i + 1) * 128, :]
    return out


# version 03 (final)
# speedup vs baseline: 1.1166x; 1.1166x over previous
import contextlib
import numpy as np
import ml_dtypes
import concourse.bass as bass
import concourse.mybir as mybir
from concourse.bass_utils import run_bass_kernel_spmd

F32 = mybir.dt.float32
BF16 = mybir.dt.bfloat16
U8 = mybir.dt.uint8
AF = mybir.ActivationFunctionType
ALU = mybir.AluOpType
AX = mybir.AxisListType
BF16NP = ml_dtypes.bfloat16

B, S, D, H = 2, 4096, 2048, 5632
NH, NKV, DH, CHUNK, NCH = 16, 4, 128, 256, 16
DB = D // 128
HB = H // 128
HBH = 22
NSLOT = 8
SCALE = DH ** -0.5
EPS_LN = 1e-5
EPS_Q = 1e-8
BIG = 1e30
NUMEL_W = H * D

W12_L = 6
SEC_W1 = W12_L * 128 * D
SEC_W3 = 2 * 128 * H
NC_ELEM = 2 * SEC_W1 + SEC_W3
OFF_W1, OFF_W2, OFF_W3 = 0, SEC_W1, 2 * SEC_W1

SEC_K = 128 * NKV * 1024
SEC_V = 128 * NKV * NSLOT * 128
KV_ELEM = SEC_K + SEC_V

_UPTO = "all"

KB = 1024
OFF_PERSIST = 0
OFF_A = 16 * KB
OFF_B = 80 * KB
OFF_C = 112 * KB
ARENA_BYTES = 188 * KB


class Op:
    __slots__ = ("sem", "idx", "fn", "waits")

    def __init__(self, sem, idx, fn, waits):
        self.sem, self.idx, self.fn, self.waits = sem, idx, fn, waits


class Tk:
    STREAMS = ("pe", "act", "dve", "pool", "sync")
    QUEUE2STREAM = {"qsync": "sync", "qact": "act", "qdve": "pool", "qpool": "pool"}

    def __init__(self):
        self.ops = {s: [] for s in self.STREAMS}
        self.idx = {}
        self.last_w = {}
        self.readers = {}
        self.waited = {}
        self.needed = set()

    def stream_of(self, sem):
        if sem.startswith("dk_"):
            raise AssertionError("dk sems have explicit streams")
        return self.QUEUE2STREAM.get(sem, {"cc": "pool"}.get(sem, sem))

    @staticmethod
    def is_dma_sem(sem):
        return sem.startswith("dk_")

    def _edge(self, waits, consumer_stream, prod):
        pe_, pi = prod
        if pe_ == "pe" and consumer_stream == "pe":
            return
        key = (consumer_stream, pe_)
        if self.waited.get(key, 0) >= pi:
            return
        self.waited[key] = pi
        self.needed.add((pe_, pi))
        waits.append((pe_, pi))

    def _record(self, stream, sem, fn, reads, writes):
        self.idx[sem] = self.idx.get(sem, 0) + 1
        i = self.idx[sem]
        waits = []
        for k in reads:
            lw = self.last_w.get(k)
            if lw is not None:
                self._edge(waits, stream, lw)
        for k in writes:
            lw = self.last_w.get(k)
            if lw is not None:
                self._edge(waits, stream, lw)
            for r in self.readers.get(k, ()):
                self._edge(waits, stream, r)
        self.ops[stream].append(Op(sem, i, fn, waits))
        for k in reads:
            self.readers.setdefault(k, []).append((sem, i))
        for k in writes:
            self.last_w[k] = (sem, i)
            self.readers[k] = []

    def op(self, sem, fn, reads=(), writes=()):
        self._record(self.stream_of(sem), sem, fn, reads, writes)

    def dma(self, q, out, in_, reads=(), writes=()):
        assert writes, "dma needs a write key"
        sem = "dk_" + writes[0]
        self._record(self.QUEUE2STREAM[q], sem, lambda eng: eng.dma_start(out=out, in_=in_),
                     reads, writes)

    def all_sems(self):
        return list(self.idx.keys())

    def barrier(self):
        for stream in self.STREAMS:
            waits = []
            for sem_ in self.all_sems():
                src_stream = None if self.is_dma_sem(sem_) else self.stream_of(sem_)
                if self.idx[sem_] > 0 and src_stream != stream:
                    self._edge(waits, stream, (sem_, self.idx[sem_]))
            if waits:
                self.ops[stream].append(Op(None, 0, None, waits))

    def fence(self):
        waits = []
        for sem_ in self.all_sems():
            if self.idx[sem_] > 0 and (self.is_dma_sem(sem_) or self.stream_of(sem_) != "pool"):
                self._edge(waits, "pool", (sem_, self.idx[sem_]))
        self.ops["pool"].append(Op(None, 0, None, waits))

    def emit_stream(self, stream, eng, sems, prefix):
        for o in self.ops[stream]:
            for (pe_, pi) in o.waits:
                step = 16 if self.is_dma_sem(pe_) else 1
                eng.wait_ge(sems[pe_], prefix[pe_][pi] * step)
            if o.fn is None:
                continue
            inst = o.fn(eng)
            if self.is_dma_sem(o.sem) or (o.sem, o.idx) in self.needed:
                step = 16 if self.is_dma_sem(o.sem) else 1
                inst.then_inc(sems[o.sem], step)

    def prefixes(self):
        prefix = {}
        for sem_ in self.all_sems():
            n = self.idx[sem_]
            pc = [0] * (n + 1)
            for i in range(1, n + 1):
                pc[i] = pc[i - 1] + (1 if ((sem_, i) in self.needed or self.is_dma_sem(sem_)) else 0)
            prefix[sem_] = pc
        return prefix


def build_program(upto):
    nc = bass.Bass("TRN2", target_bir_lowering=False, debug=False)

    IN = {}
    def di(name, shape, dtype, kind="ExternalInput", **kw):
        IN[name] = nc.dram_tensor(name, shape, dtype, kind=kind, **kw).ap()
        return IN[name]

    di("x", [NSLOT * 128, D], F32)
    di("w1s", [W12_L * 128, D], F32)
    di("w2s", [W12_L * 128, D], F32)
    di("w3s", [256, H], F32)
    di("wq", [D, NH * DH], F32)
    di("wk", [D, NKV * DH], F32)
    di("wv", [D, NKV * DH], F32)
    di("wo", [NH * DH, D], F32)
    di("id_f", [128, 128], F32)
    di("ones_f", [128, 1], F32)
    di("kidx512", [128, 512], F32)
    di("rope_cos", [128, 1024], F32)
    di("rope_sin", [128, 1024], F32)
    di("thr_own", [128, NSLOT], F32)
    di("maskpre", [NSLOT * 128, NCH], F32)
    di("maskm1b", [NSLOT * 128, NCH], F32)
    di("onehot", [NSLOT * 128, NCH], F32)
    di("maskle", [NSLOT * 128, NCH], F32)
    di("y", [NSLOT * 128, D], F32, kind="ExternalOutput")
    di("gpart", [1, 4], F32, kind="Internal")
    di("gsum", [1, 4], F32, kind="Internal")
    di("contrib", [1, NC_ELEM], BF16, kind="Internal")
    di("gat", [8, NC_ELEM], BF16, kind="Internal")
    di("kvcon", [1, KV_ELEM], BF16, kind="Internal")
    di("kvgat", [4, KV_ELEM], BF16, kind="Internal")
    di("x_res", [D, 1024], F32, kind="Internal")

    tk = Tk()
    with contextlib.ExitStack() as stack:
        arena = stack.enter_context(nc.sbuf_tensor("arena", [128, ARENA_BYTES], U8))
        base_addr = nc.lookup_mloc(arena).addr

        SB = {}
        def place(name, shape, dtype, off):
            h = nc.virtual_sbuf_tensor(name, shape, dtype)
            nc._tensor(h.name, h.shape, h.dtype, type="SB")
            mloc = nc.lookup_mloc(h)
            mloc.allocated = True
            mloc.addr = base_addr + off
            SB[name] = h.ap()
            return SB[name]

        place("idf", [128, 128], F32, 0)
        place("idb", [128, 128], BF16, 512)
        place("onesb", [128, 128], BF16, 768)
        place("onesf", [128, 1], F32, 1024)
        place("kidx", [128, 512], F32, 1056)
        place("thr_sb", [128, NSLOT], F32, 3104)
        place("selc", [128, 4 * NSLOT * NCH], F32, 3136)
        place("th_all", [128, 4], F32, 5184)
        place("thneg", [128, 4], F32, 5200)
        place("gstat", [128, 2], F32, 5216)
        place("gpsb", [1, 4], F32, 5224)
        place("gssb", [128, 4], F32, 5240)
        place("kmb", [128, NKV * NCH], F32, 5256)
        place("kscr", [128, NKV * 32], F32, 5512)
        place("kmbb", [128, NKV * NCH], BF16, 6024)
        place("gsel", [128, NH * NCH], F32, 6152)
        place("m16", [128, NH], F32, 7176)
        place("eqb", [128, NH * NCH], F32, 7240)
        place("selacc", [128, NH * NCH], F32, 8264)
        place("dcol", [128, 16], F32, 9288)
        place("dv", [128, 1], F32, 9352)
        place("rv", [128, 1], F32, 9356)
        place("rrow", [1, 128], F32, 9360)
        place("rrowb", [1, 128], BF16, 9872)
        place("rbc_sb", [128, 128], F32, 10128)
        place("pts", [128, 2 * 128], BF16, 10640)
        place("sgst", [128, 1024], BF16, 11152)
        place("sqst", [128, 1024], BF16, 13200)

        place("qslab", [128, 2 * 2048], F32, OFF_A)
        place("qb12", [128, 2048], BF16, OFF_A + 16 * KB)
        place("tst12", [128, 2048], BF16, OFF_A + 20 * KB)
        place("negm", [128, 2048], BF16, OFF_A + 24 * KB)
        place("q3slab", [128, H], F32, OFF_A + 28 * KB)
        place("qb3", [128, H], BF16, OFF_A + 50 * KB + 512)
        place("tst3", [128, H], BF16, OFF_A + 61 * KB + 512 + 512)
        place("xinst", [128, 2 * 2048], F32, OFF_A)
        place("wb1", [128, 4 * 2048], BF16, OFF_A)
        place("wb2", [128, 4 * 2048], BF16, OFF_A + 16 * KB)
        place("wb3", [128, 2 * HBH * 128], BF16, OFF_A + 32 * KB)
        place("wqbh", [128, 8 * 2048], BF16, OFF_A)
        place("wkvb", [128, DB * 1024], BF16, OFF_A)
        place("ropec", [128, 1024], F32, OFF_A + 32 * KB)
        place("ropes", [128, 1024], F32, OFF_A + 36 * KB)
        place("wfst", [128, 2 * 2048], F32, OFF_A + 40 * KB)
        place("kTg", [128, 4 * 4096], BF16, OFF_A)
        place("vg", [128, 4 * 4096], BF16, OFF_A + 32 * KB)
        place("xfin", [128, 2 * 1024], F32, OFF_A)

        place("lnT", [128, DB * 1024], BF16, OFF_B)
        place("oT", [128, NH * 1024], BF16, OFF_B)

        place("xr", [128, DB * 1024], F32, OFF_C)
        place("gT", [128, HBH * 1024], BF16, OFF_C)
        place("QT", [128, NH * 1024], BF16, OFF_C)
        place("em", [128, 4096], BF16, OFF_C + 32 * KB)
        place("kT_loc", [128, NKV * 1024], BF16, OFF_C + 32 * KB)
        place("v_loc", [128, NKV * NSLOT * 128], BF16, OFF_C + 40 * KB)
        place("wob", [128, NH * 1024], BF16, OFF_C)
        place("xst_in", [128, 2 * 1024], F32, OFF_C + 48 * KB)
        place("xst_out", [128, 2 * 1024], F32, OFF_C + 56 * KB)
        place("scrA", [128, 1024], F32, OFF_C + 64 * KB)
        place("scrB", [128, 1024], F32, OFF_C + 68 * KB)
        place("mrow", [1, 1024], F32, OFF_C + 72 * KB)
        place("vrow", [1, 1024], F32, OFF_A)
        place("mrl", [1, 1024], F32, OFF_A + 4 * KB)
        place("ostage", [128, NSLOT * 2048], F32, OFF_C)

        PS = {}
        for i, nm in enumerate(("PS0", "PS1", "PS2", "PS3")):
            PS[nm] = stack.enter_context(nc.psum_tensor(nm, [128, 1024], F32))

        emit_body(nc, tk, IN, SB, PS, upto)
        tk.fence()
        prefix = tk.prefixes()
        sems = {s: stack.enter_context(nc.semaphore("sem_" + s)) for s in tk.all_sems()}
        with nc.Block() as block:
            block.tensor(lambda eng: tk.emit_stream("pe", eng, sems, prefix))
            block.scalar(lambda eng: tk.emit_stream("act", eng, sems, prefix))
            block.vector(lambda eng: tk.emit_stream("dve", eng, sems, prefix))
            block.gpsimd(lambda eng: tk.emit_stream("pool", eng, sems, prefix))
            block.sync(lambda eng: tk.emit_stream("sync", eng, sems, prefix))
    return nc


def emit_body(nc, tk, IN, SB, PS, upto):
    dma, op, bar = tk.dma, tk.op, tk.barrier
    idf, idb, onesb, onesf = SB["idf"], SB["idb"], SB["onesb"], SB["onesf"]
    PS0, PS1, PS2, PS3 = (PS["PS0"][:, :], PS["PS1"][:, :], PS["PS2"][:, :], PS["PS3"][:, :])
    lnT = SB["lnT"]
    xst_in, xst_out = SB["xst_in"], SB["xst_out"]

    dma("qsync", idf, IN["id_f"], writes=["idf"])
    dma("qsync", onesf, IN["ones_f"], writes=["onesf"])
    op("act", lambda e: e.copy(out=idb, in_=idf), reads=["idf"], writes=["idb"])
    op("dve", lambda e: e.memset(onesb, 1.0), writes=["onesb"])
    dma("qsync", SB["kidx"], IN["kidx512"], writes=["kidx"])
    dma("qsync", SB["thr_sb"], IN["thr_own"], writes=["thr_sb"])
    for w, nm in enumerate(("maskpre", "maskm1b", "onehot", "maskle")):
        dma("qsync", SB["selc"][:, w * NSLOT * NCH:(w + 1) * NSLOT * NCH]
            .rearrange("p (s c) -> p s c", s=NSLOT),
            IN[nm].rearrange("(s p) c -> p s c", p=128), writes=["selc"])

    qslab, q3slab = SB["qslab"], SB["q3slab"]
    gstat, gpsb, gssb = SB["gstat"], SB["gpsb"], SB["gssb"]
    th_all, thneg = SB["th_all"], SB["thneg"]
    cells = {"w1": (PS2[0:1, 0:1], "PS2a"), "w2": (PS2[0:1, 512:513], "PS2b"),
             "w3": (PS3[0:1, 0:1], "PS3a")}
    for wname, src in (("w1", IN["w1s"]), ("w2", IN["w2s"])):
        for l in range(W12_L):
            sl = qslab[:, (l % 2) * 2048:(l % 2) * 2048 + 2048]
            key = f"qs{l % 2}"
            dma("qsync", sl, src[l * 128:(l + 1) * 128, :], writes=[key])
            op("dve", lambda e, sl=sl: e.reduce_sum(
                out=gstat[:, 0:1], in_=sl, axis=AX.X, apply_absolute_value=True),
                reads=[key], writes=["gstat"])
            c, ck = cells[wname]
            op("pe", lambda e, c=c, l=l: e.matmul(
                out=c, lhsT=gstat[:, 0:1], rhs=onesf, start=(l == 0), stop=(l == W12_L - 1)),
                reads=["gstat", "onesf"], writes=[ck])
    for dl in range(2):
        dma("qsync", q3slab, IN["w3s"][dl * 128:(dl + 1) * 128, :], writes=["q3s"])
        op("dve", lambda e: e.reduce_sum(
            out=gstat[:, 1:2], in_=q3slab, axis=AX.X, apply_absolute_value=True),
            reads=["q3s"], writes=["gstat3"])
        c, ck = cells["w3"]
        op("pe", lambda e, c=c, dl=dl: e.matmul(
            out=c, lhsT=gstat[:, 1:2], rhs=onesf, start=(dl == 0), stop=(dl == 1)),
            reads=["gstat3", "onesf"], writes=[ck])
    for i, wname in enumerate(("w1", "w2", "w3")):
        c, ck = cells[wname]
        op("dve", lambda e, c=c, i=i: e.tensor_copy(out=gpsb[:, i:i + 1], in_=c),
           reads=[ck], writes=["gpsb"])
    op("dve", lambda e: e.tensor_copy(out=gpsb[:, 3:4], in_=gpsb[:, 2:3]),
       reads=["gpsb"], writes=["gpsb"])
    dma("qact", IN["gpart"], gpsb, reads=["gpsb"], writes=["gpart_d"])
    op("cc", lambda e: e.collective_compute(
        "AllReduce", ALU.add, replica_groups=[list(range(8))],
        ins=[IN["gpart"]], outs=[IN["gsum"]]), reads=["gpart_d"], writes=["gsum_d"])
    dma("qact", gssb, IN["gsum"].broadcast_to([128, 4]), reads=["gsum_d"], writes=["gssb"])
    op("dve", lambda e: e.tensor_scalar(
        out=th_all, in0=gssb, scalar1=0.5 / NUMEL_W, scalar2=0.5 * EPS_Q,
        op0=ALU.mult, op1=ALU.add), reads=["gssb"], writes=["th_all"])
    op("dve", lambda e: e.tensor_scalar_mul(out=thneg, in0=th_all, scalar1=-1.0),
       reads=["th_all"], writes=["thneg"])

    qb12, tst12, negm = SB["qb12"], SB["tst12"], SB["negm"]
    contrib = IN["contrib"]
    for wname, src, woff, wi in (("w1", IN["w1s"], OFF_W1, 0), ("w2", IN["w2s"], OFF_W2, 1)):
        for l in range(W12_L):
            sl = qslab[:, (l % 2) * 2048:(l % 2) * 2048 + 2048]
            key = f"qs{l % 2}"
            dma("qsync", sl, src[l * 128:(l + 1) * 128, :], writes=[key])
            op("dve", lambda e, sl=sl, wi=wi: e.tensor_scalar(
                out=qb12, in0=sl, scalar1=th_all[:, wi:wi + 1], scalar2=None, op0=ALU.is_gt),
                reads=[key, "th_all"], writes=["qb12"])
            op("dve", lambda e, sl=sl, wi=wi: e.tensor_scalar(
                out=negm, in0=sl, scalar1=thneg[:, wi:wi + 1], scalar2=None, op0=ALU.is_lt),
                reads=[key, "thneg"], writes=["negm"])
            op("dve", lambda e: e.tensor_sub(out=qb12, in0=qb12, in1=negm),
                reads=["qb12", "negm"], writes=["qb12"])
            for k in range(DB):
                psv = (PS2[:, 0:64] if k % 2 == 0 else PS2[:, 512:576]).bitcast(BF16)
                pk = "PS2a" if k % 2 == 0 else "PS2b"
                op("pe", lambda e, k=k, psv=psv: e.transpose(
                    out=psv, in_=qb12[:, k * 128:(k + 1) * 128], identity=idb),
                    reads=["qb12", "idb"], writes=[pk])
                op("act", lambda e, k=k, psv=psv: e.copy(
                    out=tst12[:, k * 128:(k + 1) * 128], in_=psv),
                    reads=[pk], writes=["tst12"])
            dma("qact", contrib[0, woff + l * 128 * 2048: woff + (l + 1) * 128 * 2048]
                .rearrange("(p c) -> p c", p=128), tst12, reads=["tst12"], writes=["contrib_d"])
    qb3, tst3 = SB["qb3"], SB["tst3"]
    for dl in range(2):
        dma("qsync", q3slab, IN["w3s"][dl * 128:(dl + 1) * 128, :], writes=["q3s"])
        op("dve", lambda e: e.tensor_scalar(
            out=qb3, in0=q3slab, scalar1=th_all[:, 2:3], scalar2=None, op0=ALU.is_gt),
            reads=["q3s", "th_all"], writes=["qb3"])
        op("dve", lambda e: e.tensor_scalar(
            out=tst3, in0=q3slab, scalar1=thneg[:, 2:3], scalar2=None, op0=ALU.is_lt),
            reads=["q3s", "thneg"], writes=["tst3"])
        op("dve", lambda e: e.tensor_sub(out=qb3, in0=qb3, in1=tst3),
            reads=["qb3", "tst3"], writes=["qb3"])
        for hb in range(HB):
            psv = (PS2[:, 0:64] if hb % 2 == 0 else PS2[:, 512:576]).bitcast(BF16)
            pk = "PS2a" if hb % 2 == 0 else "PS2b"
            op("pe", lambda e, hb=hb, psv=psv: e.transpose(
                out=psv, in_=qb3[:, hb * 128:(hb + 1) * 128], identity=idb),
                reads=["qb3", "idb"], writes=[pk])
            op("act", lambda e, hb=hb, psv=psv: e.copy(
                out=tst3[:, hb * 128:(hb + 1) * 128], in_=psv),
                reads=[pk], writes=["tst3"])
        dma("qact", contrib[0, OFF_W3 + dl * 128 * H: OFF_W3 + (dl + 1) * 128 * H]
            .rearrange("(p c) -> p c", p=128), tst3, reads=["tst3"], writes=["contrib_d"])
    op("cc", lambda e: e.collective_compute(
        "AllGather", ALU.bypass, replica_groups=[list(range(8))],
        ins=[contrib], outs=[IN["gat"]]), reads=["contrib_d"], writes=["gat_d"])
    bar()

    xr, xinst = SB["xr"], SB["xinst"]
    for tt in range(NSLOT):
        sl = xinst[:, (tt % 2) * 2048:(tt % 2) * 2048 + 2048]
        key = f"xin{tt % 2}"
        dma("qdve", sl, IN["x"][tt * 128:(tt + 1) * 128, :], writes=[key])
        for k in range(DB):
            i2 = (tt * DB + k) % 2
            psv = PS2[:, 0:128] if i2 == 0 else PS2[:, 512:640]
            pk = "PS2a" if i2 == 0 else "PS2b"
            op("pe", lambda e, sl=sl, k=k, psv=psv: e.transpose(
                out=psv, in_=sl[:, k * 128:(k + 1) * 128], identity=idf),
                reads=[key, "idf"], writes=[pk])
            if i2 == 0:
                op("dve", lambda e, k=k, tt=tt, psv=psv: e.tensor_copy(
                    out=xr[:, k * 1024 + tt * 128: k * 1024 + tt * 128 + 128], in_=psv),
                    reads=[pk], writes=[f"xr{k}"])
            else:
                op("act", lambda e, k=k, tt=tt, psv=psv: e.copy(
                    out=xr[:, k * 1024 + tt * 128: k * 1024 + tt * 128 + 128], in_=psv),
                    reads=[pk], writes=[f"xr{k}"])
    for k in range(DB):
        dma("qact", IN["x_res"][k * 128:(k + 1) * 128, :], xr[:, k * 1024:(k + 1) * 1024],
            reads=[f"xr{k}"], writes=[f"xres{k}"])

    def emit_ln(tag, resident):
        sq = SB["sqst"]
        scell = [(PS2[0:1, 0:512], "PS2a"), (PS2[0:1, 512:1024], "PS2b"),
                 (PS3[0:1, 0:512], "PS3a"), (PS3[0:1, 512:1024], "PS3b")]
        for k in range(DB):
            if resident:
                srck, src = f"xr{k}", xr[:, k * 1024:(k + 1) * 1024]
            else:
                src = xst_in[:, (k % 2) * 1024:(k % 2) * 1024 + 1024]
                srck = f"lnst{k % 2}"
                dma("qdve", src, IN["x_res"][k * 128:(k + 1) * 128, :],
                    reads=[f"xres{k}"], writes=[srck])
            op("act", lambda e, src=src, k=k: e.copy(
                out=lnT[:, k * 1024:(k + 1) * 1024], in_=src),
                reads=[srck], writes=[f"lnT{k}"])
            op("act", lambda e, k=k: e.activation(
                out=sq, in_=lnT[:, k * 1024:(k + 1) * 1024], func=AF.Square),
                reads=[f"lnT{k}"], writes=["sqst"])
            for h2 in range(2):
                c, ck = scell[h2]
                op("pe", lambda e, k=k, h2=h2, c=c: e.matmul(
                    out=c, lhsT=onesb[:, 0:1],
                    rhs=lnT[:, k * 1024 + h2 * 512: k * 1024 + h2 * 512 + 512],
                    start=(k == 0), stop=(k == DB - 1)),
                    reads=[f"lnT{k}", "onesb"], writes=[ck])
                c, ck = scell[2 + h2]
                op("pe", lambda e, k=k, h2=h2, c=c: e.matmul(
                    out=c, lhsT=onesb[:, 0:1],
                    rhs=sq[:, h2 * 512: h2 * 512 + 512],
                    start=(k == 0), stop=(k == DB - 1)),
                    reads=["sqst", "onesb"], writes=[ck])
        mrow, vrow, mrl = SB["mrow"], SB["vrow"], SB["mrl"]
        op("dve", lambda e: e.tensor_scalar_mul(out=mrow, in0=PS2[0:1, 0:1024], scalar1=1.0 / D),
           reads=["PS2a", "PS2b"], writes=["mrow"])
        op("dve", lambda e: e.tensor_scalar_mul(out=vrow, in0=PS3[0:1, 0:1024], scalar1=1.0 / D),
           reads=["PS3a", "PS3b"], writes=["vrow"])
        op("dve", lambda e: e.tensor_mul(out=mrl, in0=mrow, in1=mrow),
           reads=["mrow"], writes=["mrl"])
        op("dve", lambda e: e.tensor_sub(out=vrow, in0=vrow, in1=mrl),
           reads=["vrow", "mrl"], writes=["vrow"])
        op("dve", lambda e: e.tensor_scalar_add(out=vrow, in0=vrow, scalar1=EPS_LN),
           reads=["vrow"], writes=["vrow"])
        op("act", lambda e: e.activation(out=vrow, in_=vrow, func=AF.Sqrt),
           reads=["vrow"], writes=["vrow"])
        op("dve", lambda e: e.reciprocal(out=vrow, in_=vrow),
           reads=["vrow"], writes=["vrow"])
        op("dve", lambda e: e.tensor_mul(out=mrl, in0=mrow, in1=vrow),
           reads=["mrow", "vrow"], writes=["mrl"])
        rlb = SB["sgst"].bitcast(BF16)[0:1, 0:1024]
        mlb = SB["sqst"].bitcast(BF16)[0:1, 0:1024]
        op("act", lambda e: e.copy(out=rlb, in_=vrow), reads=["vrow"], writes=["sgst"])
        op("act", lambda e: e.copy(out=mlb, in_=mrl), reads=["mrl"], writes=["sqst"])
        rbcl, mbcl = SB["scrA"], SB["scrB"]
        for h2 in range(2):
            c, ck = scell[h2]
            op("pe", lambda e, h2=h2: e.matmul(
                out=PS2[:, h2 * 512:h2 * 512 + 512], lhsT=onesb[0:1, :],
                rhs=rlb[:, h2 * 512:h2 * 512 + 512], start=True, stop=True),
                reads=["sgst", "onesb"], writes=["PS2a" if h2 == 0 else "PS2b"])
            op("pe", lambda e, h2=h2: e.matmul(
                out=PS3[:, h2 * 512:h2 * 512 + 512], lhsT=onesb[0:1, :],
                rhs=mlb[:, h2 * 512:h2 * 512 + 512], start=True, stop=True),
                reads=["sqst", "onesb"], writes=["PS3a" if h2 == 0 else "PS3b"])
        op("act", lambda e: e.copy(out=rbcl, in_=PS2[:, 0:1024]),
           reads=["PS2a", "PS2b"], writes=["scrA"])
        op("act", lambda e: e.copy(out=mbcl, in_=PS3[:, 0:1024]),
           reads=["PS3a", "PS3b"], writes=["scrB"])
        for k in range(DB):
            op("dve", lambda e, k=k: e.tensor_mul(
                out=lnT[:, k * 1024:(k + 1) * 1024],
                in0=lnT[:, k * 1024:(k + 1) * 1024], in1=rbcl),
                reads=[f"lnT{k}", "scrA"], writes=[f"lnT{k}"])
            op("pool", lambda e, k=k: e.tensor_sub(
                out=lnT[:, k * 1024:(k + 1) * 1024],
                in0=lnT[:, k * 1024:(k + 1) * 1024], in1=mbcl),
                reads=[f"lnT{k}", "scrB"], writes=[f"lnT{k}"])

    def emit_ffn(tag):
        wb1, wb2, wb3, sgst, gT = SB["wb1"], SB["wb2"], SB["wb3"], SB["sgst"], SB["gT"]
        gat = IN["gat"]
        for half in range(2):
            for hh in range(HBH):
                hb = half * HBH + hh
                r, l = hb % 8, hb // 8
                wsel = hh % 4
                bsel = hh % 2
                w1sl = wb1[:, wsel * 2048:wsel * 2048 + 2048]
                w2sl = wb2[:, wsel * 2048:wsel * 2048 + 2048]
                k1, k2 = f"wb1_{wsel}", f"wb2_{wsel}"
                off1 = OFF_W1 + l * 128 * 2048
                off2 = OFF_W2 + l * 128 * 2048
                dma("qsync", w1sl, gat[r, off1:off1 + 128 * 2048].rearrange("(p c) -> p c", p=128),
                    reads=["gat_d"], writes=[k1])
                dma("qsync", w2sl, gat[r, off2:off2 + 128 * 2048].rearrange("(p c) -> p c", p=128),
                    reads=["gat_d"], writes=[k2])
                u1 = [PS0, PS2][bsel]
                u2 = [PS1, PS3][bsel]
                ku1 = ["PS0", "PS2"][bsel]
                ku2 = ["PS1", "PS3"][bsel]
                for k in range(DB):
                    for h2 in range(2):
                        op("pe", lambda e, k=k, h2=h2, u1=u1, w1sl=w1sl: e.matmul(
                            out=u1[:, h2 * 512:h2 * 512 + 512],
                            lhsT=w1sl[:, k * 128:(k + 1) * 128],
                            rhs=lnT[:, k * 1024 + h2 * 512: k * 1024 + h2 * 512 + 512],
                            start=(k == 0), stop=(k == DB - 1)),
                            reads=[k1, f"lnT{k}"], writes=[ku1 + ("a" if h2 == 0 else "b")])
                        op("pe", lambda e, k=k, h2=h2, u2=u2, w2sl=w2sl: e.matmul(
                            out=u2[:, h2 * 512:h2 * 512 + 512],
                            lhsT=w2sl[:, k * 128:(k + 1) * 128],
                            rhs=lnT[:, k * 1024 + h2 * 512: k * 1024 + h2 * 512 + 512],
                            start=(k == 0), stop=(k == DB - 1)),
                            reads=[k2, f"lnT{k}"], writes=[ku2 + ("a" if h2 == 0 else "b")])
                op("act", lambda e, u1=u1: e.activation(
                    out=sgst, in_=u1, func=AF.Sigmoid),
                    reads=[ku1 + "a", ku1 + "b"], writes=["sgst"])
                op("dve", lambda e, hh=hh, u2=u2: e.tensor_mul(
                    out=gT[:, hh * 1024:(hh + 1) * 1024], in0=sgst, in1=u2),
                    reads=["sgst", ku2 + "a", ku2 + "b"], writes=[f"gT{hh}"])
            for k in range(DB):
                r, dl = k // 2, k % 2
                bsel = k % 2
                y = [PS0, PS1][bsel]
                ky = ["PS0", "PS1"][bsel]
                w3sl = wb3[:, bsel * HBH * 128:(bsel + 1) * HBH * 128]
                k3 = f"wb3_{bsel}"
                src = gat[r, OFF_W3 + dl * 128 * H: OFF_W3 + (dl + 1) * 128 * H] \
                    .rearrange("(p hb c) -> p hb c", p=128, hb=HB)[:, half * HBH:(half + 1) * HBH, :]
                dma("qsync", w3sl, src, reads=["gat_d"], writes=[k3])
                for hh in range(HBH):
                    for h2 in range(2):
                        op("pe", lambda e, hh=hh, h2=h2, y=y, w3sl=w3sl: e.matmul(
                            out=y[:, h2 * 512:h2 * 512 + 512],
                            lhsT=w3sl[:, hh * 128:(hh + 1) * 128],
                            rhs=gT[:, hh * 1024 + h2 * 512: hh * 1024 + h2 * 512 + 512],
                            start=(hh == 0), stop=(hh == HBH - 1)),
                            reads=[k3, f"gT{hh}"], writes=[ky + ("a" if h2 == 0 else "b")])
                xi = xst_in[:, bsel * 1024:bsel * 1024 + 1024]
                xo = xst_out[:, bsel * 1024:bsel * 1024 + 1024]
                kxi, kxo = f"xi{bsel}", f"xo{bsel}"
                dma("qdve", xi, IN["x_res"][k * 128:(k + 1) * 128, :],
                    reads=[f"xres{k}"], writes=[kxi])
                op("dve", lambda e, y=y, xi=xi, xo=xo: e.scalar_tensor_tensor(
                    out=xo, in0=y, scalar=0.5, in1=xi, op0=ALU.mult, op1=ALU.add),
                    reads=[ky + "a", ky + "b", kxi], writes=[kxo])
                dma("qact", IN["x_res"][k * 128:(k + 1) * 128, :], xo,
                    reads=[kxo], writes=[f"xres{k}"])
            bar()

    def emit_attn():
        ropec, ropes = SB["ropec"], SB["ropes"]
        dma("qsync", ropec, IN["rope_cos"], writes=["ropec"])
        dma("qsync", ropes, IN["rope_sin"], writes=["ropes"])
        wfst, QT = SB["wfst"], SB["QT"]
        scrA, scrB = SB["scrA"], SB["scrB"]

        def rope_evac(psrc, kps, dstap, dstkey):
            op("act", lambda e: e.copy(out=scrB[0:64, :], in_=psrc[64:128, :]),
               reads=kps, writes=["scrB"])
            op("act", lambda e: e.copy(out=scrB[64:128, :], in_=psrc[0:64, :]),
               reads=kps, writes=["scrB"])
            op("dve", lambda e: e.tensor_mul(out=scrB, in0=scrB, in1=ropes),
               reads=["scrB", "ropes"], writes=["scrB"])
            op("dve", lambda e: e.tensor_mul(out=scrA, in0=psrc, in1=ropec),
               reads=kps + ["ropec"], writes=["scrA"])
            op("pool", lambda e: e.tensor_add(out=dstap, in0=scrA, in1=scrB),
               reads=["scrA", "scrB"], writes=[dstkey])

        wqbh = SB["wqbh"]
        for hg in range(2):
            for k in range(DB):
                sl = wfst[:, (k % 2) * 2048:(k % 2) * 2048 + 1024]
                key = f"wf{k % 2}"
                dma("qsync", sl, IN["wq"][k * 128:(k + 1) * 128, hg * 1024:(hg + 1) * 1024],
                    writes=[key])
                op("pool", lambda e, sl=sl, k=k: e.tensor_copy(
                    out=wqbh[:, k * 1024:(k + 1) * 1024], in_=sl),
                    reads=[key], writes=[f"wqb{k}"])
            for h8 in range(8):
                h = hg * 8 + h8
                qps = [PS0, PS1][h8 % 2]
                kq = ["PS0", "PS1"][h8 % 2]
                for k in range(DB):
                    for h2 in range(2):
                        op("pe", lambda e, h8=h8, k=k, h2=h2, qps=qps: e.matmul(
                            out=qps[:, h2 * 512:h2 * 512 + 512],
                            lhsT=wqbh[:, k * 1024 + h8 * 128: k * 1024 + h8 * 128 + 128],
                            rhs=lnT[:, k * 1024 + h2 * 512: k * 1024 + h2 * 512 + 512],
                            start=(k == 0), stop=(k == DB - 1)),
                            reads=[f"wqb{k}", f"lnT{k}"],
                            writes=[kq + ("a" if h2 == 0 else "b")])
                rope_evac(qps, [kq + "a", kq + "b"], QT[:, h * 1024:(h + 1) * 1024], f"QT{h}")
        bar()
        wkvb = SB["wkvb"]
        for k in range(DB):
            sl = wfst[:, (k % 2) * 2048:(k % 2) * 2048 + 1024]
            key = f"wf{k % 2}"
            dma("qsync", sl[:, 0:512], IN["wk"][k * 128:(k + 1) * 128, :], writes=[key])
            dma("qsync", sl[:, 512:1024], IN["wv"][k * 128:(k + 1) * 128, :], writes=[key])
            op("pool", lambda e, sl=sl, k=k: e.tensor_copy(
                out=wkvb[:, k * 1024:(k + 1) * 1024], in_=sl), reads=[key], writes=[f"wkv{k}"])
        kT_loc, v_loc = SB["kT_loc"], SB["v_loc"]
        for kv in range(NKV):
            kps = [PS0, PS1][kv % 2]
            kk = ["PS0", "PS1"][kv % 2]
            for k in range(DB):
                for h2 in range(2):
                    op("pe", lambda e, kv=kv, k=k, h2=h2, kps=kps: e.matmul(
                        out=kps[:, h2 * 512:h2 * 512 + 512],
                        lhsT=wkvb[:, k * 1024 + kv * 128: k * 1024 + kv * 128 + 128],
                        rhs=lnT[:, k * 1024 + h2 * 512: k * 1024 + h2 * 512 + 512],
                        start=(k == 0), stop=(k == DB - 1)),
                        reads=[f"wkv{k}", f"lnT{k}"], writes=[kk + ("a" if h2 == 0 else "b")])
            rope_evac(kps, [kk + "a", kk + "b"], kT_loc[:, kv * 1024:(kv + 1) * 1024], f"kTl{kv}")
        for kv in range(NKV):
            for tt in range(NSLOT):
                i2 = (kv * NSLOT + tt) % 2
                vps = PS2[:, 0:128] if i2 == 0 else PS2[:, 512:640]
                kvp = "PS2a" if i2 == 0 else "PS2b"
                for k in range(DB):
                    op("pe", lambda e, kv=kv, tt=tt, k=k, vps=vps: e.matmul(
                        out=vps,
                        lhsT=lnT[:, k * 1024 + tt * 128: k * 1024 + tt * 128 + 128],
                        rhs=wkvb[:, k * 1024 + 512 + kv * 128: k * 1024 + 512 + kv * 128 + 128],
                        start=(k == 0), stop=(k == DB - 1)),
                        reads=[f"wkv{k}", f"lnT{k}"], writes=[kvp])
                op("act", lambda e, kv=kv, tt=tt, vps=vps: e.copy(
                    out=v_loc[:, (kv * NSLOT + tt) * 128:(kv * NSLOT + tt) * 128 + 128],
                    in_=vps), reads=[kvp], writes=[f"vl{kv}"])
        kvcon = IN["kvcon"]
        dma("qact", kvcon[0, 0:SEC_K].rearrange("(p c) -> p c", p=128), kT_loc,
            reads=["kTl0", "kTl1", "kTl2", "kTl3"], writes=["kvcon_d"])
        dma("qact", kvcon[0, SEC_K:KV_ELEM].rearrange("(p c) -> p c", p=128), v_loc,
            reads=["vl0", "vl1", "vl2", "vl3"], writes=["kvcon_d"])
        op("cc", lambda e: e.collective_compute(
            "AllGather", ALU.bypass, replica_groups=[[0, 1, 2, 3], [4, 5, 6, 7]],
            ins=[kvcon], outs=[IN["kvgat"]]), reads=["kvcon_d"], writes=["kvg_d"])
        bar()
        kTg, vg = SB["kTg"], SB["vg"]
        for r in range(4):
            dma("qsync", kTg[:, r * 4096:(r + 1) * 4096],
                IN["kvgat"][r, 0:SEC_K].rearrange("(p c) -> p c", p=128),
                reads=["kvg_d"], writes=["kTg"])
            dma("qsync", vg[:, r * 4096:(r + 1) * 4096],
                IN["kvgat"][r, SEC_K:KV_ELEM].rearrange("(p c) -> p c", p=128),
                reads=["kvg_d"], writes=["vg"])
        kmb, kscr, kmbb = SB["kmb"], SB["kscr"], SB["kmbb"]
        kview = kTg.rearrange("p (r kv l c) -> p kv l r c", r=4, kv=NKV, l=8)
        for kv in range(NKV):
            for l in range(8):
                op("dve", lambda e, kv=kv, l=l: e.reduce_sum(
                    out=kscr[:, (kv * 8 + l) * 4:(kv * 8 + l) * 4 + 4],
                    in_=kview[:, kv, l, :, :], axis=AX.X),
                    reads=["kTg"], writes=["kscr"])
        kv4 = kscr.rearrange("p (x a b) -> p x a b", a=2, b=2)
        op("dve", lambda e: e.tensor_add(
            out=kmb.rearrange("p (x a) -> p x a", a=2),
            in0=kv4[:, :, :, 0], in1=kv4[:, :, :, 1]),
            reads=["kscr"], writes=["kmb"])
        op("act", lambda e: e.copy(out=kmbb, in_=kmb), reads=["kmb"], writes=["kmbb"])

        selc = SB["selc"]
        gsel, m16, eqb, selacc = SB["gsel"], SB["m16"], SB["eqb"], SB["selacc"]
        dcol, dv, rv = SB["dcol"], SB["dv"], SB["rv"]
        em, oT = SB["em"], SB["oT"]
        kidx, thr_sb = SB["kidx"], SB["thr_sb"]
        gv = gsel.rearrange("p (h c) -> p h c", h=NH)
        ev = eqb.rearrange("p (h c) -> p h c", h=NH)
        sv3 = selacc.rearrange("p (h c) -> p h c", h=NH)
        for i in range(NSLOT):
            gate_ps = PS2[:, 0:256]
            for h in range(NH):
                kv = h // 4
                op("pe", lambda e, i=i, h=h, kv=kv: e.matmul(
                    out=gate_ps[:, h * NCH:(h + 1) * NCH],
                    lhsT=QT[:, h * 1024 + i * 128: h * 1024 + i * 128 + 128],
                    rhs=kmbb[:, kv * NCH:(kv + 1) * NCH],
                    start=True, stop=True, skip_group_check=True),
                    reads=[f"QT{h}", "kmbb"], writes=["PS2a"])
            mpre = selc[:, (0 * NSLOT + i) * NCH:(0 * NSLOT + i + 1) * NCH]
            m1b = selc[:, (1 * NSLOT + i) * NCH:(1 * NSLOT + i + 1) * NCH]
            moh = selc[:, (2 * NSLOT + i) * NCH:(2 * NSLOT + i + 1) * NCH]
            mle = selc[:, (3 * NSLOT + i) * NCH:(3 * NSLOT + i + 1) * NCH]
            op("dve", lambda e, mpre=mpre, gate_ps=gate_ps: e.tensor_tensor(
                out=gv, in0=gate_ps.rearrange("p (h c) -> p h c", h=NH),
                in1=mpre[:, None, :].broadcast_to([128, NH, NCH]), op=ALU.mult),
                reads=["PS2a", "selc"], writes=["gsel"])
            op("dve", lambda e, m1b=m1b: e.tensor_tensor(
                out=gv, in0=gv, in1=m1b[:, None, :].broadcast_to([128, NH, NCH]),
                op=ALU.add), reads=["gsel", "selc"], writes=["gsel"])
            for it in range(3):
                op("dve", lambda e: e.reduce_max(out=m16, in_=gv, axis=AX.X),
                   reads=["gsel"], writes=["m16"])
                op("dve", lambda e: e.tensor_tensor(
                    out=ev, in0=gv, in1=m16[:, :, None].broadcast_to([128, NH, NCH]),
                    op=ALU.is_ge), reads=["gsel", "m16"], writes=["eqb"])
                if it == 0:
                    op("dve", lambda e: e.tensor_copy(out=selacc, in_=eqb),
                       reads=["eqb"], writes=["selacc"])
                else:
                    op("dve", lambda e: e.tensor_add(out=selacc, in0=selacc, in1=eqb),
                       reads=["selacc", "eqb"], writes=["selacc"])
                if it < 2:
                    op("dve", lambda e: e.scalar_tensor_tensor(
                        out=gsel, in0=eqb, scalar=-BIG, in1=gsel, op0=ALU.mult, op1=ALU.add),
                        reads=["eqb", "gsel"], writes=["gsel"])
            op("pool", lambda e, moh=moh: e.tensor_tensor(
                out=sv3, in0=sv3, in1=moh[:, None, :].broadcast_to([128, NH, NCH]),
                op=ALU.add), reads=["selacc", "selc"], writes=["selacc"])
            op("dve", lambda e: e.tensor_scalar_min(out=selacc, in0=selacc, scalar1=1.0),
               reads=["selacc"], writes=["selacc"])
            op("pool", lambda e, mle=mle: e.tensor_tensor(
                out=sv3, in0=sv3, in1=mle[:, None, :].broadcast_to([128, NH, NCH]),
                op=ALU.mult), reads=["selacc", "selc"], writes=["selacc"])
            nkb = i + 1
            nkt = 4 * (i + 1)
            for h in range(NH):
                kv = h // 4
                scps = [(PS0[:, 0:512], "PS0a"), (PS0[:, 512:1024], "PS0b"),
                        (PS1[:, 0:512], "PS1a"), (PS1[:, 512:1024], "PS1b")]
                for kb in range(nkb):
                    sps, ksp = scps[kb % 4]
                    rhs = kview[:, kv, kb, :, :]
                    op("pe", lambda e, i=i, h=h, sps=sps, rhs=rhs: e.matmul(
                        out=sps, lhsT=QT[:, h * 1024 + i * 128: h * 1024 + i * 128 + 128],
                        rhs=rhs, start=True, stop=True),
                        reads=[f"QT{h}", "kTg"], writes=[ksp])
                    op("act", lambda e, kb=kb, sps=sps: e.activation(
                        out=em[:, kb * 512:(kb + 1) * 512], in_=sps, func=AF.Exp, scale=SCALE),
                        reads=[ksp], writes=["em"])
                op("dve", lambda e, i=i: e.scalar_tensor_tensor(
                    out=em[:, i * 512:(i + 1) * 512], in0=kidx,
                    scalar=thr_sb[:, i:i + 1],
                    in1=em[:, i * 512:(i + 1) * 512], op0=ALU.is_le, op1=ALU.mult),
                    reads=["em", "thr_sb", "kidx"], writes=["em"])
                for kb in range(nkb):
                    sv = sv3[:, h, 2 * kb:2 * kb + 2]
                    op("dve", lambda e, kb=kb, sv=sv: e.scalar_tensor_tensor(
                        out=em[:, kb * 512:(kb + 1) * 512].rearrange("p (c j) -> p c j", c=2),
                        in0=em[:, kb * 512:(kb + 1) * 512].rearrange("p (c j) -> p c j", c=2),
                        scalar=1.0, in1=sv[:, :, None].broadcast_to([128, 2, 256]),
                        op0=ALU.mult, op1=ALU.mult,
                        accum_out=dcol[:, kb:kb + 1]),
                        reads=["em", "selacc"], writes=["em", "dcolk"])
                op("dve", lambda e, nkb=nkb: e.reduce_sum(out=dv, in_=dcol[:, 0:nkb], axis=AX.X),
                   reads=["dcolk"], writes=["dv"])
                op("dve", lambda e: e.reciprocal(out=rv, in_=dv), reads=["dv"], writes=["rv"])
                op("pe", lambda e: e.transpose(out=PS3[0:1, 128:256], in_=rv, identity=idf),
                   reads=["rv", "idf"], writes=["PS3a"])
                op("dve", lambda e: e.tensor_copy(out=SB["rrow"], in_=PS3[0:1, 128:256]),
                   reads=["PS3a"], writes=["rrow"])
                op("act", lambda e: e.copy(out=SB["rrowb"], in_=SB["rrow"]),
                   reads=["rrow"], writes=["rrowb"])
                op("pe", lambda e: e.matmul(
                    out=PS3[:, 512:640], lhsT=onesb[0:1, :], rhs=SB["rrowb"],
                    start=True, stop=True), reads=["rrowb", "onesb"], writes=["PS3b"])
                op("act", lambda e: e.copy(out=SB["rbc_sb"], in_=PS3[:, 512:640]),
                   reads=["PS3b"], writes=["rbc_sb"])
                pts = SB["pts"]
                for kt in range(nkt):
                    i2 = kt % 2
                    ppsv = (PS2[:, 0:64] if i2 == 0 else PS2[:, 512:576]).bitcast(BF16)
                    kpp = "PS2a" if i2 == 0 else "PS2b"
                    op("pe", lambda e, kt=kt, ppsv=ppsv: e.transpose(
                        out=ppsv, in_=em[:, kt * 128:(kt + 1) * 128], identity=idb),
                        reads=["em", "idb"], writes=[kpp])
                    pt = pts[:, i2 * 128:i2 * 128 + 128]
                    kpt = f"pts{i2}"
                    if i2 == 0:
                        op("act", lambda e, ppsv=ppsv, pt=pt: e.copy(out=pt, in_=ppsv),
                           reads=[kpp], writes=[kpt])
                    else:
                        op("dve", lambda e, ppsv=ppsv, pt=pt: e.tensor_copy(out=pt, in_=ppsv),
                           reads=[kpp], writes=[kpt])
                    r_, l_ = kt % 4, kt // 4
                    vt = vg[:, r_ * 4096 + kv * 1024 + l_ * 128:
                            r_ * 4096 + kv * 1024 + l_ * 128 + 128]
                    op("pe", lambda e, kt=kt, vt=vt, pt=pt, nkt=nkt: e.matmul(
                        out=PS3[:, 0:128], lhsT=vt, rhs=pt,
                        start=(kt == 0), stop=(kt == nkt - 1)),
                        reads=["vg", kpt], writes=["PS3a"])
                op("dve", lambda e, h=h, i=i: e.tensor_mul(
                    out=oT[:, h * 1024 + i * 128: h * 1024 + i * 128 + 128],
                    in0=PS3[:, 0:128], in1=SB["rbc_sb"]),
                    reads=["PS3a", "rbc_sb"], writes=[f"oT{h}"])
        bar()
        wob = SB["wob"]
        for dh2 in range(2):
            for h in range(NH):
                sl = wfst[:, (h % 2) * 2048:(h % 2) * 2048 + 1024]
                key = f"wf{h % 2}"
                dma("qsync", sl, IN["wo"][h * 128:(h + 1) * 128, dh2 * 1024:(dh2 + 1) * 1024],
                    writes=[key])
                op("pool", lambda e, sl=sl, h=h: e.tensor_copy(
                    out=wob[:, h * 1024:(h + 1) * 1024], in_=sl),
                    reads=[key], writes=[f"wob{h}"])
            for k8 in range(8):
                k = dh2 * 8 + k8
                bsel = k8 % 2
                aps = [PS0, PS1][bsel]
                kap = ["PS0", "PS1"][bsel]
                for h in range(NH):
                    for h2 in range(2):
                        op("pe", lambda e, h=h, h2=h2, k8=k8, aps=aps: e.matmul(
                            out=aps[:, h2 * 512:h2 * 512 + 512],
                            lhsT=wob[:, h * 1024 + k8 * 128: h * 1024 + k8 * 128 + 128],
                            rhs=oT[:, h * 1024 + h2 * 512: h * 1024 + h2 * 512 + 512],
                            start=(h == 0), stop=(h == NH - 1)),
                            reads=[f"wob{h}", f"oT{h}"],
                            writes=[kap + ("a" if h2 == 0 else "b")])
                xi = xst_in[:, bsel * 1024:bsel * 1024 + 1024]
                xo = xst_out[:, bsel * 1024:bsel * 1024 + 1024]
                kxi, kxo = f"xi{bsel}", f"xo{bsel}"
                dma("qdve", xi, IN["x_res"][k * 128:(k + 1) * 128, :],
                    reads=[f"xres{k}"], writes=[kxi])
                op("dve", lambda e, aps=aps, xi=xi, xo=xo: e.scalar_tensor_tensor(
                    out=xo, in0=aps, scalar=1.0, in1=xi, op0=ALU.mult, op1=ALU.add),
                    reads=[kap + "a", kap + "b", kxi], writes=[kxo])
                dma("qact", IN["x_res"][k * 128:(k + 1) * 128, :], xo,
                    reads=[kxo], writes=[f"xres{k}"])

    def emit_out():
        ostage, xfin = SB["ostage"], SB["xfin"]
        for k in range(DB):
            xi = xfin[:, (k % 2) * 1024:(k % 2) * 1024 + 1024]
            kxi = f"fxi{k % 2}"
            dma("qdve", xi, IN["x_res"][k * 128:(k + 1) * 128, :],
                reads=[f"xres{k}"], writes=[kxi])
            for tt in range(NSLOT):
                i2 = (k * NSLOT + tt) % 2
                psv = PS2[:, 0:128] if i2 == 0 else PS2[:, 512:640]
                pk = "PS2a" if i2 == 0 else "PS2b"
                op("pe", lambda e, xi=xi, tt=tt, psv=psv: e.transpose(
                    out=psv, in_=xi[:, tt * 128:(tt + 1) * 128], identity=idf),
                    reads=[kxi, "idf"], writes=[pk])
                if i2 == 0:
                    op("act", lambda e, tt=tt, k=k, psv=psv: e.copy(
                        out=ostage[:, tt * 2048 + k * 128: tt * 2048 + k * 128 + 128],
                        in_=psv), reads=[pk], writes=[f"ost{tt}"])
                else:
                    op("dve", lambda e, tt=tt, k=k, psv=psv: e.tensor_copy(
                        out=ostage[:, tt * 2048 + k * 128: tt * 2048 + k * 128 + 128],
                        in_=psv), reads=[pk], writes=[f"ost{tt}"])
        for tt in range(NSLOT):
            dma("qact", IN["y"][tt * 128:(tt + 1) * 128, :],
                ostage[:, tt * 2048:(tt + 1) * 2048], reads=[f"ost{tt}"], writes=["y_d"])

    emit_ln("l1", resident=True)
    bar()
    emit_ffn("f1")
    bar()
    if upto != "ffn1":
        emit_ln("l2", resident=False)
        bar()
        emit_attn()
        bar()
    if upto == "all":
        emit_ln("l3", resident=False)
        bar()
        emit_ffn("f2")
        bar()
    emit_out()



def _host_constants(core):
    b, j = core // 4, core % 4
    tiles = [4 * i + j for i in range(NSLOT)]
    pos = np.concatenate([np.arange(t * 128, (t + 1) * 128) for t in tiles])
    ident = np.eye(128, dtype=np.float32)
    out = {
        "id_f": ident,
        "ones_f": np.ones((128, 1), dtype=np.float32),
        "kidx512": np.broadcast_to(np.arange(512, dtype=np.float32), (128, 512)).copy(),
    }
    invf = (1.0 / (10000.0 ** (np.arange(0, DH, 2, dtype=np.float64) / DH)))
    ang = pos.astype(np.float64)[None, :] * invf[:, None]
    cos = np.cos(ang)
    sin = np.sin(ang)
    out["rope_cos"] = np.concatenate([cos, cos], 0).astype(np.float32)
    rs = np.concatenate([-sin, sin], 0).astype(np.float32)
    out["rope_sin"] = rs
    thr = np.zeros((128, NSLOT), dtype=np.float32)
    for i in range(NSLOT):
        thr[:, i] = pos[i * 128:(i + 1) * 128] - 512.0 * i
    out["thr_own"] = thr
    mpre = np.zeros((NSLOT * 128, NCH), dtype=np.float32)
    m1b = np.zeros((NSLOT * 128, NCH), dtype=np.float32)
    moh = np.zeros((NSLOT * 128, NCH), dtype=np.float32)
    mle = np.zeros((NSLOT * 128, NCH), dtype=np.float32)
    for i in range(NSLOT):
        qc = tiles[i] // 2
        c = np.arange(NCH)
        mpre[i * 128:(i + 1) * 128, :] = (c < qc).astype(np.float32)[None, :]
        m1b[i * 128:(i + 1) * 128, :] = (((c < qc).astype(np.float32) - 1.0) * BIG)[None, :]
        moh[i * 128:(i + 1) * 128, :] = (c == qc).astype(np.float32)[None, :]
        mle[i * 128:(i + 1) * 128, :] = (c <= qc).astype(np.float32)[None, :]
    out["maskpre"] = mpre
    out["maskm1b"] = m1b
    out["onehot"] = moh
    out["maskle"] = mle
    return out, tiles


def kernel(L0_attn_hidden_states, L0_attn_wq, L0_attn_wk, L0_attn_wv, L0_attn_wo,
           L0_ffn_w1, L0_ffn_w2, L0_ffn_w3):
    x = np.ascontiguousarray(np.asarray(L0_attn_hidden_states, dtype=np.float32))
    wq = np.ascontiguousarray(np.asarray(L0_attn_wq, dtype=np.float32))
    wk = np.ascontiguousarray(np.asarray(L0_attn_wk, dtype=np.float32))
    wv = np.ascontiguousarray(np.asarray(L0_attn_wv, dtype=np.float32))
    wo = np.ascontiguousarray(np.asarray(L0_attn_wo, dtype=np.float32))
    w1 = np.ascontiguousarray(np.asarray(L0_ffn_w1, dtype=np.float32))
    w2 = np.ascontiguousarray(np.asarray(L0_ffn_w2, dtype=np.float32))
    w3 = np.ascontiguousarray(np.asarray(L0_ffn_w3, dtype=np.float32))

    nc = build_program(_UPTO)
    in_maps = []
    tile_maps = []
    for core in range(8):
        b, j = core // 4, core % 4
        consts, tiles = _host_constants(core)
        tile_maps.append((b, tiles))
        xs = np.concatenate([x[b, t * 128:(t + 1) * 128, :] for t in tiles], axis=0)
        def w12_slice(w):
            rows = []
            for l in range(W12_L):
                jj = l * 8 + core
                if jj < HB:
                    rows.append(w[jj * 128:(jj + 1) * 128, :])
                else:
                    rows.append(np.zeros((128, w.shape[1]), dtype=np.float32))
            return np.ascontiguousarray(np.concatenate(rows, axis=0))
        m = {
            "x": np.ascontiguousarray(xs),
            "w1s": w12_slice(w1),
            "w2s": w12_slice(w2),
            "w3s": np.ascontiguousarray(w3[core * 256:(core + 1) * 256, :]),
            "wq": wq, "wk": wk, "wv": wv, "wo": wo,
        }
        m.update(consts)
        in_maps.append(m)
    res = run_bass_kernel_spmd(nc, in_maps, core_ids=list(range(8)))
    out = np.zeros((B, S, D), dtype=np.float32)
    for core in range(8):
        b, tiles = tile_maps[core]
        yc = np.asarray(res.results[core]["y"])
        for i, t in enumerate(tiles):
            out[b, t * 128:(t + 1) * 128, :] = yc[i * 128:(i + 1) * 128, :]
    return out
